# version 02
# baseline (speedup 1.0000x reference)
import numpy as np
import ml_dtypes
import concourse.bass as bass
import concourse.mybir as mybir
from concourse.bass_utils import run_bass_kernel_spmd

F32 = mybir.dt.float32
BF16 = mybir.dt.bfloat16
AF = mybir.ActivationFunctionType
ALU = mybir.AluOpType
AX = mybir.AxisListType.X

NCORES = 8
D = 1024
DB = 8
ED = 2048
EG = 16
NST = 16
RK = 64
NEXP = 8
FF = 2048
FB = 16
LPC = 1024
B = 2
T = B * LPC
HALO = 3
TH = LPC + HALO
THT = B * TH
LN_EPS = 1e-5
RMS_EPS = 1e-6

ENGINES = ("pe", "act", "dve", "pool", "sp")
QUEUES = ("qpe", "qact", "qdve", "qpool", "qsp")
ENG2Q = {"pe": "qpe", "act": "qact", "dve": "qdve", "pool": "qpool", "sp": "qsp"}


class Op:
    __slots__ = ("eng", "dom", "emit", "deps", "marked", "midx", "is_dma", "tag", "sem_name")

    def __init__(self, eng, dom, emit, is_dma, tag):
        self.eng = eng
        self.dom = dom
        self.emit = emit
        self.deps = []
        self.marked = False
        self.midx = 0
        self.is_dma = is_dma
        self.tag = tag


class Prog:

    def __init__(self):
        self.ops = []
        self.by_eng = {e: [] for e in ENGINES}
        self.last_w = {}
        self.readers = {}

    def _dep(self, op, prod, force=False):
        if prod is op:
            return
        if (not force) and (not prod.is_dma) and prod.eng == op.eng \
                and not op.is_dma:
            return
        op.deps.append(prod)

    def add(self, eng, emit, reads=(), writes=(), dma=False, tag="", mark=False,
            sync_reads=()):
        dom = ENG2Q[eng] if dma else eng
        op = Op(eng, dom, emit, dma, tag)
        op.marked = mark
        for r in sync_reads:
            w = self.last_w.get(r)
            if w is not None:
                self._dep(op, w, force=True)
            self.readers.setdefault(r, []).append(op)
        for r in reads:
            w = self.last_w.get(r)
            if w is not None:
                self._dep(op, w)
            self.readers.setdefault(r, []).append(op)
        for r in writes:
            w = self.last_w.get(r)
            if w is not None:
                self._dep(op, w)
            for rd in self.readers.get(r, ()):
                self._dep(op, rd)
            self.last_w[r] = op
            self.readers[r] = []
        self.ops.append(op)
        self.by_eng[eng].append(op)
        return op

    def dma(self, eng, out, in_, reads=(), writes=(), tag="", mark=False, **kw):
        return self.add(
            eng,
            lambda p, out=out, in_=in_, kw=kw: p.dma_start(out=out, in_=in_, **kw),
            reads=reads, writes=writes, dma=True, tag=tag, mark=mark)

    def finalize(self, nc, block, sems, final_eng="pool"):
        import os
        NQS = int(os.environ.get("KNQS", "8"))
        for op in self.ops:
            for d in op.deps:
                d.marked = True
        eng_count = {}
        qstate = {}
        fence_before = {}
        for op in self.ops:
            if op.is_dma:
                q = op.dom
                st = qstate.setdefault(q, {"epoch": 0, "tot": [0] * NQS})
                k = st["epoch"] % NQS
                if st["epoch"] > 0 and st["epoch"] % NQS == 0 and st["tot"][k] > 0 \
                        and st.get("fence_at") != st["epoch"]:
                    fence_before[op] = [(f"{q}{j}", st["tot"][j])
                                        for j in range(NQS) if st["tot"][j] > 0]
                    st["fence_at"] = st["epoch"]
                st["tot"][k] += 1
                op.sem_name = f"{q}{k}"
                op.midx = st["tot"][k]
                if op.marked:
                    st["epoch"] += 1
            else:
                if op.marked:
                    eng_count[op.eng] = eng_count.get(op.eng, 0) + 1
                    op.midx = eng_count[op.eng]
                    op.sem_name = op.eng

        def emit_engine(eng, proxy):
            know = {}
            for op in self.by_eng[eng]:
                for (sn, v) in fence_before.get(op, ()):
                    if know.get(sn, 0) < v:
                        proxy.wait_ge(sems[sn], v * 16)
                        know[sn] = v
                for dprod in op.deps:
                    sn = dprod.sem_name
                    v = dprod.midx
                    if know.get(sn, 0) < v:
                        proxy.wait_ge(sems[sn], v * (16 if dprod.is_dma else 1))
                        know[sn] = v
                inst = op.emit(proxy)
                if op.is_dma:
                    inst.then_inc(sems[op.sem_name], 16)
                elif op.marked:
                    inst.then_inc(sems[op.sem_name], 1)
            if eng == final_eng:
                for q, st in qstate.items():
                    for j in range(len(st["tot"])):
                        if st["tot"][j] > 0 and know.get(f"{q}{j}", 0) < st["tot"][j]:
                            proxy.wait_ge(sems[f"{q}{j}"], st["tot"][j] * 16)
                for e, cnt in eng_count.items():
                    if e != eng and know.get(e, 0) < cnt:
                        proxy.wait_ge(sems[e], cnt)

        @block.tensor
        def _(p):
            emit_engine("pe", p)

        @block.scalar
        def _(p):
            emit_engine("act", p)

        @block.vector
        def _(p):
            emit_engine("dve", p)

        @block.gpsimd
        def _(p):
            emit_engine("pool", p)

        @block.sync
        def _(p):
            emit_engine("sp", p)


class Ctx:

    def __init__(self, nc):
        self.nc = nc
        self.prog = Prog()
        self.A = nc.alloc_sbuf_tensor("poolA", [128, 16384], F32)
        self.Bp = nc.alloc_sbuf_tensor("poolB", [128, 12288], F32)
        self.C = nc.alloc_sbuf_tensor("poolC", [128, 16384], F32)
        self.Ab = self.A.bitcast(BF16)
        self.Bb = self.Bp.bitcast(BF16)
        self.Cb = self.C.bitcast(BF16)
        self.wstg = nc.alloc_sbuf_tensor("wstg", [128, 1024], F32)
        self.rows = nc.alloc_sbuf_tensor("rows", [128, 1536], F32)
        self.rows_dram = nc.dram_tensor("rows_dram", [16, 2240], F32,
                                        kind="Internal")
        self.PS = [nc.alloc_psum_tensor(f"psb{i}", [128, 512], F32)
                   for i in range(6)]
        self.cst = nc.alloc_sbuf_tensor("cst", [128, 8], F32)
        self.ones_bf = nc.alloc_sbuf_tensor("ones_bf", [128, 2], BF16)
        p = self.prog
        p.add("dve", lambda pr: pr.memset(self.ones_bf[:, :], 1.0),
              writes=("ones",))
        p.add("dve", lambda pr: pr.memset(self.cst[:, 0:1], 1.0),
              writes=("cst",))

    @staticmethod
    def rkeys(buf, col0, ncols, esz):
        b0 = col0 * esz
        b1 = b0 + ncols * esz
        return tuple(f"{buf}:{i}" for i in range(b0 // 8192, (b1 - 1) // 8192 + 1))


def chunks(total, step, base=0):
    out = []
    o = 0
    while o < total:
        out.append((base + o, min(step, total - o)))
        o += step
    return out


def ln_stats(ctx, tag, get_bf, nblk, cols, r0, eps, sub_mean, psa_i, psb_i,
             sq_tile, sq_keys, rows_cols=None):
    p = ctx.prog
    rows = ctx.rows
    psA, psB = ctx.PS[psa_i], ctx.PS[psb_i]
    pka, pkb = f"ps{psa_i}", f"ps{psb_i}"
    rk = f"rows:{r0}"
    rkd = rk + "d"
    inv = 1.0 / (nblk * 128)
    mu = rows[0:1, 0:512]
    rst = rows[0:1, 512:1024]
    tmp = rows[0:1, 1024:1536]
    for (o, n) in chunks(cols, 512):
        for i in range(nblk):
            ap, keys = get_bf(i)
            st, sp = (i == 0), (i == nblk - 1)
            p.add("pe", lambda pr, a=ap, o=o, n=n, st=st, sp=sp:
                  pr.matmul(psA[0:1, 0:n], ctx.ones_bf[:, 0:1], a[:, o:o + n],
                            start=st, stop=sp),
                  reads=keys + ("ones",), writes=(pka,))
        for i in range(nblk):
            ap, keys = get_bf(i)
            p.add("act", lambda pr, a=ap, o=o, n=n: pr.activation(
                out=sq_tile[:, o:o + n], in_=a[:, o:o + n], func=AF.Square),
                reads=keys, writes=sq_keys)
            st, sp = (i == 0), (i == nblk - 1)
            p.add("pe", lambda pr, o=o, n=n, st=st, sp=sp:
                  pr.matmul(psB[0:1, 0:n], ctx.ones_bf[:, 0:1],
                            sq_tile[:, o:o + n], start=st, stop=sp),
                  reads=sq_keys + ("ones",), writes=(pkb,))
        p.add("dve", lambda pr, o=o, n=n: pr.tensor_scalar_mul(
            mu[:, 0:n], psA[0:1, 0:n], inv),
            reads=(pka,), writes=(rk,))
        p.add("dve", lambda pr, o=o, n=n: pr.tensor_scalar_mul(
            tmp[:, 0:n], psB[0:1, 0:n], inv),
            reads=(pkb,), writes=(rk,))
        if sub_mean:
            p.add("dve", lambda pr, o=o, n=n: pr.scalar_tensor_tensor(
                out=rst[:, 0:n], in0=mu[:, 0:n], scalar=-1.0,
                in1=mu[:, 0:n], op0=ALU.mult, op1=ALU.mult),
                reads=(rk,), writes=(rk,))
            p.add("dve", lambda pr, o=o, n=n: pr.tensor_add(
                tmp[:, 0:n], tmp[:, 0:n], rst[:, 0:n]),
                reads=(rk,), writes=(rk,))
        p.add("dve", lambda pr, o=o, n=n: pr.tensor_scalar_add(
            tmp[:, 0:n], tmp[:, 0:n], float(eps)),
            reads=(rk,), writes=(rk,))
        p.add("act", lambda pr, o=o, n=n: pr.activation(
            out=tmp[:, 0:n], in_=tmp[:, 0:n], func=AF.Sqrt),
            reads=(rk,), writes=(rk,))
        p.add("dve", lambda pr, o=o, n=n: pr.reciprocal(
            rst[:, 0:n], tmp[:, 0:n]),
            reads=(rk,), writes=(rk,))
        if sub_mean:
            p.add("dve", lambda pr, o=o, n=n: pr.tensor_mul(
                mu[:, 0:n], mu[:, 0:n], rst[:, 0:n]),
                reads=(rk,), writes=(rk,))
        p.dma("sp", ctx.rows_dram.ap()[r0:r0 + 1, o:o + n], mu[:, 0:n],
              reads=(rk,), writes=(rkd,), tag="rows_spill")
        p.dma("sp", ctx.rows_dram.ap()[r0 + 1:r0 + 2, o:o + n], rst[:, 0:n],
              reads=(rk,), writes=(rkd,), tag="rows_spill")
    return rk


def bcast_rows(ctx, r0, row, o, n, dst, dst_keys, q="sp"):
    src = ctx.rows_dram.ap()[r0 + row:r0 + row + 1, o:o + n].to_broadcast([128, n])
    ctx.prog.dma(q, dst[:, 0:n], src, reads=(f"rows:{r0}d",),
                 writes=dst_keys, tag="bcast")


def _t(a):
    return np.ascontiguousarray(np.asarray(a).T.astype(np.float32, copy=False))


def _pcol(v):
    v = np.asarray(v).astype(np.float32, copy=False)
    nb = v.shape[0] // 128
    return np.ascontiguousarray(v.reshape(nb, 128).T)


def prep_host(inputs):
    W = {}
    W["rms1_w"] = _pcol(inputs["L0_attn_rms1_w"])
    W["rms2_w"] = _pcol(inputs["L0_attn_rms2_w"])
    W["W_inT"] = _t(inputs["L0_attn_W_in"])
    cw = np.asarray(inputs["L0_attn_conv_w"]).reshape(ED, 4)
    W["conv_w"] = np.ascontiguousarray(
        cw.reshape(EG, 128, 4).transpose(1, 0, 2).reshape(128, EG * 4))
    W["conv_b"] = _pcol(inputs["L0_attn_conv_b"])
    W["W_xprojT"] = _t(inputs["L0_attn_W_xproj"])
    W["W_dtT"] = _t(inputs["L0_attn_W_dt"])
    al = np.asarray(inputs["L0_attn_A_log"])
    W["A_log"] = np.ascontiguousarray(
        al.reshape(EG, 128, NST).transpose(1, 0, 2).reshape(128, EG * NST))
    W["b_dt"] = _pcol(inputs["L0_attn_b_dt"])
    W["D_skip"] = _pcol(inputs["L0_attn_D_skip"])
    W["W_outT"] = _t(inputs["L0_attn_W_out"])
    W["W_routerT"] = _t(inputs["L0_attn_W_router"])
    W["gate_wT"] = np.ascontiguousarray(
        np.asarray(inputs["L0_attn_gate_w"]).astype(np.float32).transpose(0, 2, 1))
    W["up_wT"] = np.ascontiguousarray(
        np.asarray(inputs["L0_attn_up_w"]).astype(np.float32).transpose(0, 2, 1))
    W["down_wT"] = np.ascontiguousarray(
        np.asarray(inputs["L0_attn_down_w"]).astype(np.float32).transpose(0, 2, 1))
    W["W_rT"] = _t(inputs["L0_ffn_W_r"])
    W["W_kT"] = _t(inputs["L0_ffn_W_k"])
    W["W_vT"] = _t(inputs["L0_ffn_W_v"])
    W["mu_r"] = _pcol(inputs["L0_ffn_mu_r"])
    W["mu_k"] = _pcol(inputs["L0_ffn_mu_k"])
    W["w_qkvT"] = _t(inputs["L1_attn_w_qkv"])
    W["b_qkv"] = _pcol(inputs["L1_attn_b_qkv"])
    idx = np.arange(D)
    perm = (idx % 8) * 128 + idx // 8
    W["w_oT"] = np.ascontiguousarray(_t(inputs["L1_attn_w_o"])[perm])
    W["b_o"] = _pcol(inputs["L1_attn_b_o"])
    W["ln1_w"] = _pcol(inputs["L1_ffn_ln1_weight"])
    W["ln1_b"] = _pcol(inputs["L1_ffn_ln1_bias"])
    W["ln2_w"] = _pcol(inputs["L1_ffn_ln2_weight"])
    W["ln2_b"] = _pcol(inputs["L1_ffn_ln2_bias"])
    W["qkv_wT"] = _t(inputs["L1_ffn_qkv_weight"])
    W["proj_wT"] = _t(inputs["L1_ffn_proj_weight"])
    W["router_wT"] = _t(inputs["L1_ffn_router_weight"])
    W["moe_w1"] = np.ascontiguousarray(
        np.asarray(inputs["L1_ffn_moe_w1"]).astype(np.float32))
    W["moe_w2"] = np.ascontiguousarray(
        np.asarray(inputs["L1_ffn_moe_w2"]).astype(np.float32))
    c = np.arange(D)
    W["headmask"] = np.ascontiguousarray(
        ((np.arange(128)[:, None] % 8) == (c[None, :] % 8)).astype(np.float32))

    x = np.asarray(inputs["L0_attn_x"]).astype(np.float32)
    xt = np.ascontiguousarray(x.transpose(2, 0, 1))
    percore = []
    for cidx in range(NCORES):
        arr = np.zeros((D, B, TH), dtype=np.float32)
        lo = cidx * LPC
        arr[:, :, HALO:] = xt[:, :, lo:lo + LPC]
        if cidx > 0:
            arr[:, :, :HALO] = xt[:, :, lo - HALO:lo]
        arr = np.ascontiguousarray(
            arr.reshape(DB, 128, B, TH).transpose(1, 0, 2, 3))
        cm = np.zeros((128, 8), dtype=np.float32)
        cm[:, :cidx] = 1.0
        sp = np.zeros((128, 8), dtype=np.float32)
        if cidx > 0:
            sp[:, cidx - 1] = 1.0
        percore.append({"xs_in": arr, "cmask": cm, "selprev": sp})
    return W, percore


def finish_output(shards):
    out = np.empty((B, NCORES * LPC, D), dtype=np.float32)
    for c, a in enumerate(shards):
        v = a.reshape(128, DB, B, LPC).transpose(2, 3, 1, 0).reshape(B, LPC, D)
        out[:, c * LPC:(c + 1) * LPC, :] = v
    return np.ascontiguousarray(out)


def new_launch():
    nc = bass.Bass("TRN2", target_bir_lowering=False,
                   detect_race_conditions=False)
    return nc


def make_sems(nc):
    doms = list(ENGINES) + [f"{q}{k}" for q in QUEUES for k in range(8)]
    return {d: nc.semaphore("sem_" + d).__enter__() for d in doms}


def run_launch(nc, ctx, in_maps, trace=False):
    sems = make_sems(nc)
    with nc.Block() as block:
        ctx.prog.finalize(nc, block, sems)
    res = run_bass_kernel_spmd(nc, in_maps, core_ids=list(range(NCORES)),
                               trace=trace)
    return res


def stream_to_cores(x):
    xt = np.ascontiguousarray(
        np.asarray(x).astype(np.float32).transpose(2, 0, 1))
    out = []
    for c in range(NCORES):
        sl = xt[:, :, c * LPC:(c + 1) * LPC]
        a = sl.reshape(DB, 128, B, LPC).transpose(1, 0, 2, 3)
        out.append(np.ascontiguousarray(a.reshape(128, DB, T)))
    return out


def bnd_from_cores(xs_cores):
    bnds = []
    for c in range(NCORES):
        b = np.zeros((128, DB, B), np.float32)
        if c > 0:
            prev = xs_cores[c - 1].reshape(128, DB, B, LPC)
            b[:, :, :] = prev[:, :, :, LPC - 1]
        bnds.append(np.ascontiguousarray(b))
    return bnds


def run_l0f(xs_cores, W):
    bnds = bnd_from_cores(xs_cores)
    nc = new_launch()
    ctx = Ctx(nc)
    build_l0f(nc, ctx)
    in_maps = [{"xs": xs_cores[c], "bnd": bnds[c],
                "W_rT": W["W_rT"], "W_kT": W["W_kT"], "W_vT": W["W_vT"],
                "mu_r": W["mu_r"], "mu_k": W["mu_k"]} for c in range(NCORES)]
    res = run_launch(nc, ctx, in_maps)
    return [r["xso"] for r in res.results]


def load_w_bf(ctx, hbm_ap, nrows, ncols, dst, dc0, dkeys, q="act",
              cvt_eng="act", stg_half=0):
    p = ctx.prog
    sc = 512
    s0 = stg_half * 512
    for (o, n) in chunks(ncols, sc):
        skey = (f"wstg:{stg_half}",)
        p.dma(q, ctx.wstg[0:nrows, s0:s0 + n], hbm_ap[0:nrows, o:o + n],
              reads=(), writes=skey, tag="wld")
        if cvt_eng == "act":
            p.add(cvt_eng, lambda pr, o=o, n=n, r=nrows: pr.copy(
                out=dst[0:r, dc0 + o:dc0 + o + n], in_=ctx.wstg[0:r, s0:s0 + n]),
                reads=skey, writes=dkeys, tag="wcvt")
        else:
            p.add(cvt_eng, lambda pr, o=o, n=n, r=nrows: pr.tensor_copy(
                out=dst[0:r, dc0 + o:dc0 + o + n], in_=ctx.wstg[0:r, s0:s0 + n]),
                reads=skey, writes=dkeys, tag="wcvt")


def build_l0f(nc, ctx):
    p = ctx.prog
    g = {}
    g["xs"] = nc.dram_tensor("xs", [128, DB, T], F32, kind="ExternalInput").ap()
    g["bnd"] = nc.dram_tensor("bnd", [128, DB, B], F32, kind="ExternalInput").ap()
    for nm in ("W_rT", "W_kT", "W_vT"):
        g[nm] = nc.dram_tensor(nm, [D, D], F32, kind="ExternalInput").ap()
    g["mu_r"] = nc.dram_tensor("mu_r", [128, DB], F32, kind="ExternalInput").ap()
    g["mu_k"] = nc.dram_tensor("mu_k", [128, DB], F32, kind="ExternalInput").ap()
    xso = nc.dram_tensor("xso", [128, DB, T], F32, kind="ExternalOutput").ap()

    A, Bf, C = ctx.A, ctx.Bp, ctx.C
    Ab, Bb, Cb = ctx.Ab, ctx.Bb, ctx.Cb
    mu_r = nc.alloc_sbuf_tensor("mu_r_t", [128, DB], F32)
    mu_k = nc.alloc_sbuf_tensor("mu_k_t", [128, DB], F32)
    omr = nc.alloc_sbuf_tensor("omr_t", [128, DB], F32)
    omk = nc.alloc_sbuf_tensor("omk_t", [128, DB], F32)
    bndt = nc.alloc_sbuf_tensor("bnd_t", [128, DB * B], F32)
    f0b = nc.alloc_sbuf_tensor("f0b", [128, DB * B], F32)
    brb = nc.alloc_sbuf_tensor("brb", [128, 2 * B], F32)
    rs_bc = nc.alloc_sbuf_tensor("rs_bc", [128, 512], F32)
    mr_bc = nc.alloc_sbuf_tensor("mr_bc", [128, 512], F32)
    prevc = nc.alloc_sbuf_tensor("prevc", [128, 1], F32)
    xrl = nc.alloc_sbuf_tensor("xrl", [128, 1024], F32)
    bbf = nc.alloc_sbuf_tensor("bnd_bf", [128, DB * B], BF16)
    bsq = nc.alloc_sbuf_tensor("bnd_sq", [128, DB * B], BF16)

    p.dma("sp", mu_r[:, :], g["mu_r"], writes=("mur",))
    p.dma("sp", mu_k[:, :], g["mu_k"], writes=("muk",))
    p.dma("sp", bndt[:, :], g["bnd"].rearrange("p c b -> p (c b)"),
          writes=("bnd",))
    p.add("dve", lambda pr: pr.scalar_tensor_tensor(
        out=omr[:, :], in0=mu_r[:, :], scalar=-1.0,
        in1=ctx.cst[:, 0:1].to_broadcast([128, DB]),
        op0=ALU.mult, op1=ALU.add), reads=("mur", "cst"), writes=("omr",))
    p.add("dve", lambda pr: pr.scalar_tensor_tensor(
        out=omk[:, :], in0=mu_k[:, :], scalar=-1.0,
        in1=ctx.cst[:, 0:1].to_broadcast([128, DB]),
        op0=ALU.mult, op1=ALU.add), reads=("muk", "cst"), writes=("omk",))

    for i in range(DB):
        p.dma("sp", A[:, i * T:(i + 1) * T], g["xs"][:, i, :],
              writes=Ctx.rkeys("A", i * T, T, 4), tag="xload")
        p.add("dve", lambda pr, i=i: pr.tensor_copy(
            out=Cb[:, i * T:(i + 1) * T], in_=A[:, i * T:(i + 1) * T]),
            reads=Ctx.rkeys("A", i * T, T, 4),
            writes=Ctx.rkeys("C", i * T, T, 2))

    def xa_bf(i):
        return Cb[:, i * T:(i + 1) * T], Ctx.rkeys("C", i * T, T, 2)

    sqt = Cb[:, DB * T:DB * T + T]
    sq_keys = Ctx.rkeys("C", DB * T, T, 2)
    ln_stats(ctx, "lnc", xa_bf, DB, T, 0, LN_EPS, True, 0, 1, sqt, sq_keys)

    p.add("dve", lambda pr: pr.tensor_copy(out=bbf[:, :], in_=bndt[:, :]),
          reads=("bnd",), writes=("bndbf",))
    p.add("act", lambda pr: pr.activation(out=bsq[:, :], in_=bbf[:, :],
                                          func=AF.Square),
          reads=("bndbf",), writes=("bndsq",))
    bps, bpk = ctx.PS[2], "ps2"
    bpq, bpqk = ctx.PS[3], "ps3"
    for i in range(DB):
        st, sp_ = (i == 0), (i == DB - 1)
        p.add("pe", lambda pr, i=i, st=st, sp_=sp_: pr.matmul(
            bps[0:1, 0:B], ctx.ones_bf[:, 0:1], bbf[:, i * B:(i + 1) * B],
            start=st, stop=sp_), reads=("bndbf", "ones"), writes=(bpk,))
    for i in range(DB):
        st, sp_ = (i == 0), (i == DB - 1)
        p.add("pe", lambda pr, i=i, st=st, sp_=sp_: pr.matmul(
            bpq[0:1, 0:B], ctx.ones_bf[:, 0:1], bsq[:, i * B:(i + 1) * B],
            start=st, stop=sp_), reads=("bndsq", "ones"), writes=(bpqk,))
    bmu = ctx.rows[0:1, 0:512]
    brs = ctx.rows[0:1, 512:1024]
    bvar = ctx.rows[0:1, 1024:1536]
    rkb = "rows:0"
    p.add("dve", lambda pr: pr.tensor_scalar_mul(bmu[:, 0:B], bps[0:1, 0:B],
                                                 1.0 / D),
          reads=(bpk,), writes=(rkb,))
    p.add("dve", lambda pr: pr.tensor_scalar_mul(bvar[:, 0:B], bpq[0:1, 0:B],
                                                 1.0 / D),
          reads=(bpqk,), writes=(rkb,))
    p.add("dve", lambda pr: pr.scalar_tensor_tensor(
        out=brs[:, 0:B], in0=bmu[:, 0:B], scalar=-1.0, in1=bmu[:, 0:B],
        op0=ALU.mult, op1=ALU.mult), reads=(rkb,), writes=(rkb,))
    p.add("dve", lambda pr: pr.tensor_add(bvar[:, 0:B], bvar[:, 0:B],
                                          brs[:, 0:B]),
          reads=(rkb,), writes=(rkb,))
    p.add("dve", lambda pr: pr.tensor_scalar_add(bvar[:, 0:B], bvar[:, 0:B],
                                                 LN_EPS),
          reads=(rkb,), writes=(rkb,))
    p.add("act", lambda pr: pr.activation(out=bvar[:, 0:B], in_=bvar[:, 0:B],
                                          func=AF.Sqrt),
          reads=(rkb,), writes=(rkb,))
    p.add("dve", lambda pr: pr.reciprocal(brs[:, 0:B], bvar[:, 0:B]),
          reads=(rkb,), writes=(rkb,))
    p.dma("sp", ctx.rows_dram.ap()[6:7, 0:B], bmu[:, 0:B],
          reads=(rkb,), writes=("rows:6d",))
    p.dma("sp", ctx.rows_dram.ap()[7:8, 0:B], brs[:, 0:B],
          reads=(rkb,), writes=("rows:6d",))
    p.dma("sp", brb[:, 0:B],
          ctx.rows_dram.ap()[6:7, 0:B].to_broadcast([128, B]),
          reads=("rows:6d",), writes=("brb",))
    p.dma("sp", brb[:, B:2 * B],
          ctx.rows_dram.ap()[7:8, 0:B].to_broadcast([128, B]),
          reads=("rows:6d",), writes=("brb",))
    rs16 = nc.alloc_sbuf_tensor("rs16", [128, DB * B], F32)
    for b in range(B):
        for i in range(DB):
            col = i * B + b
            p.add("dve", lambda pr, col=col, b=b: pr.tensor_sub(
                f0b[:, col:col + 1], bndt[:, col:col + 1], brb[:, b:b + 1]),
                reads=("bnd", "brb"), writes=("f0b",))
            p.add("dve", lambda pr, col=col, b=b: pr.tensor_copy(
                out=rs16[:, col:col + 1], in_=brb[:, B + b:B + b + 1]),
                reads=("brb",), writes=("rs16",))
    p.add("dve", lambda pr: pr.tensor_mul(f0b[:, :], f0b[:, :], rs16[:, :]),
          sync_reads=("f0b", "rs16"), writes=("f0b",))

    dbg = nc.dram_tensor("dbg", [128, 64], F32, kind="ExternalOutput").ap()
    p.dma("act", dbg[:, 0:16], f0b[:, :], reads=("f0b",), writes=("dbg",),
          mark=True)
    f0c = Bf[:, 0:520]
    f0keys = Ctx.rkeys("B", 0, 520, 4)
    tv = Bf[:, 520:1032]
    tkeys = Ctx.rkeys("B", 520, 512, 4)
    for i in range(DB):
        xa_k = Ctx.rkeys("A", i * T, T, 4)
        for b in range(B):
            for (o, n) in chunks(LPC, 512, base=b * LPC):
                first = (o == b * LPC)
                if not first:
                    p.add("dve", lambda pr: pr.tensor_copy(
                        out=prevc[:, 0:1], in_=f0c[:, 512:513]),
                        reads=f0keys, writes=("prevc",))
                bcast_rows(ctx, 0, 1, o, n, rs_bc, ("rsbc",))
                bcast_rows(ctx, 0, 0, o, n, mr_bc, ("mrbc",))
                if first:
                    p.add("dve", lambda pr, i=i, b=b: pr.tensor_copy(
                        out=f0c[:, 0:1], in_=f0b[:, i * B + b:i * B + b + 1]),
                        sync_reads=("f0b",), writes=f0keys)
                else:
                    p.add("dve", lambda pr: pr.tensor_copy(
                        out=f0c[:, 0:1], in_=prevc[:, 0:1]),
                        sync_reads=("prevc",), writes=f0keys)
                p.add("dve", lambda pr, i=i, o=o, n=n: pr.tensor_mul(
                    f0c[:, 1:n + 1], A[:, i * T + o:i * T + o + n],
                    rs_bc[:, 0:n]), reads=xa_k + ("rsbc",), writes=f0keys)
                p.add("dve", lambda pr, n=n: pr.tensor_sub(
                    f0c[:, 1:n + 1], f0c[:, 1:n + 1], mr_bc[:, 0:n]),
                    reads=("mrbc",) + f0keys, writes=f0keys)
                xr_keys = Ctx.rkeys("C", i * T + o, n, 2)
                xk_keys = Ctx.rkeys("C", DB * T + i * T + o, n, 2)
                p.add("dve", lambda pr, i=i, n=n: pr.tensor_scalar_mul(
                    tv[:, 0:n], f0c[:, 1:n + 1], mu_r[:, i:i + 1]),
                    reads=f0keys + ("mur",), writes=tkeys)
                p.add("dve", lambda pr, i=i, o=o, n=n: pr.scalar_tensor_tensor(
                    out=Cb[:, i * T + o:i * T + o + n], in0=f0c[:, 0:n],
                    scalar=omr[:, i:i + 1], in1=tv[:, 0:n], op0=ALU.mult,
                    op1=ALU.add), reads=tkeys + ("omr",),
                    sync_reads=f0keys, writes=xr_keys)
                p.add("dve", lambda pr, i=i, n=n: pr.tensor_scalar_mul(
                    tv[:, 0:n], f0c[:, 1:n + 1], mu_k[:, i:i + 1]),
                    reads=f0keys + ("muk",), writes=tkeys)
                p.add("dve", lambda pr, i=i, o=o, n=n: pr.scalar_tensor_tensor(
                    out=Cb[:, DB * T + i * T + o:DB * T + i * T + o + n],
                    in0=f0c[:, 0:n], scalar=omk[:, i:i + 1], in1=tv[:, 0:n],
                    op0=ALU.mult, op1=ALU.add),
                    reads=f0keys + tkeys + ("omk",), writes=xk_keys)

    wk0 = 0
    wkeys = Ctx.rkeys("B", wk0, 8192, 2)
    ka0 = 8192
    kakeys = Ctx.rkeys("B", ka0, DB * T, 2)

    def mm_block(wname, rhs_off_fn, rhs_keys_fn, rhs_view, epil):
        for kb in range(DB):
            load_w_bf(ctx, g[wname][kb * 128:(kb + 1) * 128, :], 128, D,
                      Bb, wk0 + kb * D, wkeys, q="act", stg_half=kb % 2)
        for mb in range(DB):
            for ci, (o, n) in enumerate(chunks(T, 512)):
                pi = 3 + ((mb * 4 + ci) % 2)
                ps, psk = ctx.PS[pi], f"ps{pi}"
                for kb in range(DB):
                    st, sp_ = (kb == 0), (kb == DB - 1)
                    p.add("pe", lambda pr, kb=kb, mb=mb, o=o, n=n, st=st,
                          sp_=sp_, ps=ps: pr.matmul(
                              ps[:, 0:n],
                              Bb[:, wk0 + kb * D + mb * 128:
                                 wk0 + kb * D + (mb + 1) * 128],
                              rhs_view[:, rhs_off_fn(kb) + o:
                                       rhs_off_fn(kb) + o + n],
                              start=st, stop=sp_),
                          reads=wkeys + rhs_keys_fn(kb, o, n), writes=(psk,))
                epil(mb, o, n, ps, psk)

    rsc = Ctx.rkeys("A", 0, 512, 4)

    def ep_k(mb, o, n, ps, psk):
        p.add("act", lambda pr, mb=mb, o=o, n=n, ps=ps: pr.activation(
            out=ctx.A[:, 0:n], in_=ps[:, 0:n], func=AF.Relu),
            reads=(psk,), writes=rsc)
        p.add("act", lambda pr, mb=mb, o=o, n=n: pr.activation(
            out=Bb[:, ka0 + mb * T + o:ka0 + mb * T + o + n],
            in_=ctx.A[:, 0:n], func=AF.Square),
            reads=rsc, writes=kakeys)

    mm_block("W_kT", lambda kb: DB * T + kb * T,
             lambda kb, o, n: Ctx.rkeys("C", DB * T + kb * T + o, n, 2),
             Cb, ep_k)

    def ep_r(mb, o, n, ps, psk):
        p.add("act", lambda pr, mb=mb, o=o, n=n, ps=ps: pr.activation(
            out=A[:, mb * T + o:mb * T + o + n], in_=ps[:, 0:n],
            func=AF.Sigmoid),
            reads=(psk,), writes=Ctx.rkeys("A", mb * T + o, n, 4))

    mm_block("W_rT", lambda kb: kb * T,
             lambda kb, o, n: Ctx.rkeys("C", kb * T + o, n, 2), Cb, ep_r)

    def ep_v(mb, o, n, ps, psk):
        ak = Ctx.rkeys("A", mb * T + o, n, 4)
        p.add("dve", lambda pr, mb=mb, o=o, n=n, ps=ps: pr.tensor_mul(
            A[:, mb * T + o:mb * T + o + n],
            A[:, mb * T + o:mb * T + o + n], ps[:, 0:n]),
            reads=(psk,) + ak, writes=ak)

    mm_block("W_vT", lambda kb: ka0 + kb * T,
             lambda kb, o, n: Ctx.rkeys("B", ka0 + kb * T + o, n, 2), Bb, ep_v)

    for i in range(DB):
        p.add("dve", lambda pr, i=i: pr.tensor_copy(
            out=Cb[:, i * T:(i + 1) * T], in_=A[:, i * T:(i + 1) * T]),
            reads=Ctx.rkeys("A", i * T, T, 4),
            writes=Ctx.rkeys("C", i * T, T, 2))

    ln_stats(ctx, "lnd", xa_bf, DB, T, 3, LN_EPS, True, 0, 1, sqt, sq_keys)

    for i in range(DB):
        for ci, (o, n) in enumerate(chunks(T, 512)):
            ak = Ctx.rkeys("A", i * T + o, n, 4)
            sl = ci % 2
            p.dma("sp", xrl[:, sl * 512:sl * 512 + n], g["xs"][:, i, o:o + n],
                  writes=(f"xrl{sl}",), tag="xreload")
            bcast_rows(ctx, 3, 1, o, n, rs_bc, ("rsbc",))
            bcast_rows(ctx, 3, 0, o, n, mr_bc, ("mrbc",))
            p.add("dve", lambda pr, i=i, o=o, n=n: pr.tensor_mul(
                A[:, i * T + o:i * T + o + n], A[:, i * T + o:i * T + o + n],
                rs_bc[:, 0:n]), reads=ak + ("rsbc",), writes=ak)
            p.add("dve", lambda pr, i=i, o=o, n=n: pr.tensor_sub(
                A[:, i * T + o:i * T + o + n], A[:, i * T + o:i * T + o + n],
                mr_bc[:, 0:n]), reads=ak + ("mrbc",), writes=ak)
            p.add("dve", lambda pr, i=i, o=o, n=n, sl=sl: pr.tensor_add(
                A[:, i * T + o:i * T + o + n], A[:, i * T + o:i * T + o + n],
                xrl[:, sl * 512:sl * 512 + n]),
                reads=ak + (f"xrl{sl}",), writes=ak)
            p.dma("act", xso[:, i, o:o + n], A[:, i * T + o:i * T + o + n],
                  reads=ak, writes=(f"xso{i}_{o}",), tag="xso", mark=True)
    return g


def _ln_apply_bf16(ctx, r0, src_f32_fn, dst_fn, tmp, tmp_keys, cols=T):
    p = ctx.prog
    rs_bc = ctx.nc.alloc_sbuf_tensor(f"rsb{r0}_{len(ctx.prog.ops)}", [128, 512], F32)
    mr_bc = ctx.nc.alloc_sbuf_tensor(f"mrb{r0}_{len(ctx.prog.ops)}", [128, 512], F32)
    ku = f"lnb{r0}_{len(ctx.prog.ops)}"
    for i in range(DB):
        for (o, n) in chunks(cols, 512):
            bcast_rows(ctx, r0, 1, o, n, rs_bc, (ku + "r",))
            bcast_rows(ctx, r0, 0, o, n, mr_bc, (ku + "m",))
            sap, skeys = src_f32_fn(i, o, n)
            dap, dkeys = dst_fn(i, o, n)
            p.add("dve", lambda pr, s=sap, n=n: pr.tensor_mul(
                tmp[:, 0:n], s, rs_bc[:, 0:n]),
                reads=skeys + (ku + "r",), writes=tmp_keys)
            p.add("dve", lambda pr, d=dap, n=n: pr.tensor_sub(
                d, tmp[:, 0:n], mr_bc[:, 0:n]),
                reads=tmp_keys + (ku + "m",), writes=dkeys)


def build_l1a1(nc, ctx):
    p = ctx.prog
    g = {}
    g["xs"] = nc.dram_tensor("xs", [128, DB, T], F32, kind="ExternalInput").ap()
    g["w_qkvT"] = nc.dram_tensor("w_qkvT", [D, 3 * D], F32,
                                 kind="ExternalInput").ap()
    g["b_qkv"] = nc.dram_tensor("b_qkv", [128, 24], F32,
                                kind="ExternalInput").ap()
    gout = nc.dram_tensor("gout", [128, DB, B, D], F32,
                          kind="ExternalOutput").ap()
    qd = nc.dram_tensor("qd", [128, DB * T], BF16, kind="ExternalOutput").ap()

    A, Ab, Bb, Cb = ctx.A, ctx.Ab, ctx.Bb, ctx.Cb
    bq = nc.alloc_sbuf_tensor("bq_t", [128, 24], F32)
    tmp = nc.alloc_sbuf_tensor("l1a_tmp", [128, 512], F32)
    gst = nc.alloc_sbuf_tensor("l1a_gst", [128, 1024], F32)
    idt = nc.alloc_sbuf_tensor("idt", [128, 128], BF16)
    psb1 = nc.alloc_psum_tensor("psbf1", [128, 128], BF16)
    psb2 = nc.alloc_psum_tensor("psbf2", [128, 128], BF16)
    ident = nc.inline_tensor(
        np.eye(128, dtype=np.float32).astype(ml_dtypes.bfloat16),
        name="identbf").ap()
    p.dma("sp", idt[:, :], ident, writes=("idt",))
    p.dma("sp", bq[:, :], g["b_qkv"], writes=("bq",))

    for i in range(DB):
        p.dma("sp", A[:, i * T:(i + 1) * T], g["xs"][:, i, :],
              writes=Ctx.rkeys("A", i * T, T, 4), tag="xload")
        p.add("dve", lambda pr, i=i: pr.tensor_copy(
            out=Cb[:, i * T:(i + 1) * T], in_=A[:, i * T:(i + 1) * T]),
            reads=Ctx.rkeys("A", i * T, T, 4),
            writes=Ctx.rkeys("C", i * T, T, 2))

    def xa_bf(i):
        return Cb[:, i * T:(i + 1) * T], Ctx.rkeys("C", i * T, T, 2)

    sqt = Cb[:, DB * T:DB * T + T]
    sq_keys = Ctx.rkeys("C", DB * T, T, 2)
    ln_stats(ctx, "lne", xa_bf, DB, T, 0, LN_EPS, True, 0, 1, sqt, sq_keys)

    def src_x(i, o, n):
        return (A[:, i * T + o:i * T + o + n],
                Ctx.rkeys("A", i * T + o, n, 4))

    def dst_a0(i, o, n):
        return (Cb[:, i * T + o:i * T + o + n],
                Ctx.rkeys("C", i * T + o, n, 2))

    _ln_apply_bf16(ctx, 0, src_x, dst_a0, tmp, ("l1a_tmp",))

    wkeys = Ctx.rkeys("B", 0, 24576, 2)
    for kb in range(DB):
        load_w_bf(ctx, g["w_qkvT"][kb * 128:(kb + 1) * 128, :], 128, 3 * D,
                  Bb, kb * 3 * D, wkeys, q="act", stg_half=kb % 2)

    for mb in range(24):
        if mb < 16:
            c0 = mb * T
            dstv, dkpre = Ab, "A"
        else:
            c0 = DB * T + (mb - 16) * T
            dstv, dkpre = Cb, "C"
        for ci, (o, n) in enumerate(chunks(T, 512)):
            pi = 4 + ((mb * 4 + ci) % 2)
            ps, psk = ctx.PS[pi], f"ps{pi}"
            for kb in range(DB):
                st, sp_ = (kb == 0), (kb == DB - 1)
                p.add("pe", lambda pr, kb=kb, mb=mb, o=o, n=n, st=st, sp_=sp_,
                      ps=ps: pr.matmul(
                          ps[:, 0:n],
                          Bb[:, kb * 3 * D + mb * 128:kb * 3 * D + (mb + 1) * 128],
                          Cb[:, kb * T + o:kb * T + o + n],
                          start=st, stop=sp_),
                      reads=wkeys + Ctx.rkeys("C", kb * T + o, n, 2),
                      writes=(psk,))
            p.add("act", lambda pr, mb=mb, c0=c0, o=o, n=n, ps=ps, dstv=dstv:
                  pr.activation(out=dstv[:, c0 + o:c0 + o + n], in_=ps[:, 0:n],
                                func=AF.Identity, bias=bq[:, mb:mb + 1]),
                  reads=(psk, "bq"), writes=Ctx.rkeys(dkpre, c0 + o, n, 2))

    for (o, n) in chunks(DB * T, 4096):
        p.dma("act", qd[:, o:o + n], Ab[:, o:o + n],
              reads=Ctx.rkeys("A", o, n, 2), writes=(f"qd{o}",), mark=True)

    for b in range(B):
        for (srcv, spre, src0, dst0) in ((Ab, "A", DB * T, 0),
                                         (Cb, "C", DB * T, 8192)):
            for cb in range(DB):
                for lb in range(DB):
                    ps_t = psb1 if (cb * DB + lb) % 2 == 0 else psb2
                    pk = "psbf1" if (cb * DB + lb) % 2 == 0 else "psbf2"
                    sc0 = src0 + cb * T + b * LPC + lb * 128
                    p.add("pe", lambda pr, sc0=sc0, ps=ps_t, srcv=srcv:
                          pr.transpose(ps[:, :], srcv[:, sc0:sc0 + 128],
                                       idt[:, :]),
                          reads=Ctx.rkeys(spre, sc0, 128, 2) + ("idt",),
                          writes=(pk,))
                    dc0 = dst0 + lb * D + cb * 128
                    p.add("act", lambda pr, dc0=dc0, ps=ps_t: pr.copy(
                        out=Bb[:, dc0:dc0 + 128], in_=ps[:, :]),
                        reads=(pk,), writes=Ctx.rkeys("B", dc0, 128, 2))
        for cb in range(DB):
            for ci, (o, n) in enumerate(chunks(D, 512)):
                pi = 4 + ((cb * 2 + ci) % 2)
                ps, psk = ctx.PS[pi], f"ps{pi}"
                for lb in range(DB):
                    st, sp_ = (lb == 0), (lb == DB - 1)
                    p.add("pe", lambda pr, cb=cb, lb=lb, o=o, n=n, st=st,
                          sp_=sp_, ps=ps: pr.matmul(
                              ps[:, 0:n],
                              Bb[:, lb * D + cb * 128:lb * D + (cb + 1) * 128],
                              Bb[:, 8192 + lb * D + o:8192 + lb * D + o + n],
                              start=st, stop=sp_),
                          reads=Ctx.rkeys("B", 0, 16384, 2), writes=(psk,))
                p.add("act", lambda pr, o=o, n=n, ps=ps: pr.copy(
                    out=gst[:, o:o + n], in_=ps[:, 0:n]),
                    reads=(psk,), writes=("l1a_gst",))
            p.dma("act", gout[:, cb, b, :], gst[:, 0:D],
                  reads=("l1a_gst",), writes=(f"gout{b}_{cb}",), mark=True)
    return g


def build_l1a2(nc, ctx):
    p = ctx.prog
    g = {}
    g["xs"] = nc.dram_tensor("xs", [128, DB, T], F32, kind="ExternalInput").ap()
    g["gstk"] = nc.dram_tensor("gstk", [NCORES, 128, DB, B, D], F32,
                               kind="ExternalInput").ap()
    g["qd"] = nc.dram_tensor("qd", [128, DB * T], BF16,
                             kind="ExternalInput").ap()
    g["w_oT"] = nc.dram_tensor("w_oT", [D, D], F32, kind="ExternalInput").ap()
    g["b_o"] = nc.dram_tensor("b_o", [128, DB], F32, kind="ExternalInput").ap()
    g["headmask"] = nc.dram_tensor("headmask", [128, D], F32,
                                   kind="ExternalInput").ap()
    xso = nc.dram_tensor("xso", [128, DB, T], F32, kind="ExternalOutput").ap()

    A, Ab, Bf, Bb, Cb = ctx.A, ctx.Ab, ctx.Bp, ctx.Bb, ctx.Cb
    bo = nc.alloc_sbuf_tensor("bo_t", [128, DB], F32)
    maskt = nc.alloc_sbuf_tensor("maskt", [128, D], F32)
    tmp = nc.alloc_sbuf_tensor("l1a2_tmp", [128, 512], F32)
    rs_bc = nc.alloc_sbuf_tensor("l1a2_rs", [128, 512], F32)
    mr_bc = nc.alloc_sbuf_tensor("l1a2_mr", [128, 512], F32)
    xrl = nc.alloc_sbuf_tensor("l1a2_xrl", [128, 1024], F32)
    p.dma("sp", bo[:, :], g["b_o"], writes=("bo",))
    p.dma("sp", maskt[:, :], g["headmask"], writes=("maskt",))

    gflat = g["gstk"].rearrange("r p c b d -> r p (c b d)")
    for (o, n) in chunks(16384, 2048):
        for r in range(NCORES):
            sl = r % 2
            p.dma("sp", Bf[:, sl * 2048:sl * 2048 + n], gflat[r, :, o:o + n],
                  writes=(f"B:{sl}", f"B:{sl}gs"), tag="gld")
            if r == 0:
                p.add("dve", lambda pr, o=o, n=n, sl=sl: pr.tensor_copy(
                    out=A[:, o:o + n], in_=Bf[:, sl * 2048:sl * 2048 + n]),
                    reads=(f"B:{sl}gs",), writes=Ctx.rkeys("A", o, n, 4))
            else:
                p.add("dve", lambda pr, o=o, n=n, sl=sl: pr.tensor_add(
                    A[:, o:o + n], A[:, o:o + n],
                    Bf[:, sl * 2048:sl * 2048 + n]),
                    reads=(f"B:{sl}gs",) + Ctx.rkeys("A", o, n, 4),
                    writes=Ctx.rkeys("A", o, n, 4))

    for cb in range(DB):
        for b in range(B):
            c0 = cb * 2048 + b * D
            d0 = (cb * B + b) * D
            p.add("dve", lambda pr, c0=c0, d0=d0: pr.tensor_mul(
                Cb[:, d0:d0 + D], A[:, c0:c0 + D], maskt[:, :]),
                reads=Ctx.rkeys("A", c0, D, 4) + ("maskt",),
                writes=Ctx.rkeys("C", d0, D, 2))

    for (o, n) in chunks(DB * T, 4096):
        p.dma("sp", Cb[:, 16384 + o:16384 + o + n], g["qd"][:, o:o + n],
              writes=Ctx.rkeys("C", 16384 + o, n, 2), tag="qld")

    for b in range(B):
        for cpb in range(DB):
            for ci, (o, n) in enumerate(chunks(LPC, 512)):
                pi = 4 + ((cpb * 2 + ci) % 2)
                ps, psk = ctx.PS[pi], f"ps{pi}"
                for cb in range(DB):
                    st, sp_ = (cb == 0), (cb == DB - 1)
                    lc = (cb * B + b) * D + cpb * 128
                    rc = 16384 + cb * T + b * LPC + o
                    p.add("pe", lambda pr, lc=lc, rc=rc, n=n, st=st, sp_=sp_,
                          ps=ps: pr.matmul(
                              ps[:, 0:n], Cb[:, lc:lc + 128],
                              Cb[:, rc:rc + n], start=st, stop=sp_),
                          reads=Ctx.rkeys("C", lc, 128, 2) +
                          Ctx.rkeys("C", rc, n, 2), writes=(psk,))
                d0 = b * 8192 + cpb * LPC + o
                p.add("act", lambda pr, d0=d0, n=n, ps=ps: pr.copy(
                    out=Ab[:, d0:d0 + n], in_=ps[:, 0:n]),
                    reads=(psk,), writes=Ctx.rkeys("A", d0, n, 2))

    wk = Ctx.rkeys("B", 8192, 8192, 2)
    for kb in range(DB):
        load_w_bf(ctx, g["w_oT"][kb * 128:(kb + 1) * 128, :], 128, D,
                  Bb, 8192 + kb * D, wk, q="act", stg_half=kb % 2)

    sqt = Bb[:, 16384:16384 + 1024]
    sqk = Ctx.rkeys("B", 16384, 1024, 2)
    for b in range(B):
        for mb in range(DB):
            for ci, (o, n) in enumerate(chunks(LPC, 512)):
                pi = 4 + ((mb * 2 + ci) % 2)
                ps, psk = ctx.PS[pi], f"ps{pi}"
                for kb in range(DB):
                    st, sp_ = (kb == 0), (kb == DB - 1)
                    rc = b * 8192 + kb * LPC + o
                    p.add("pe", lambda pr, kb=kb, mb=mb, rc=rc, n=n, st=st,
                          sp_=sp_, ps=ps: pr.matmul(
                              ps[:, 0:n],
                              Bb[:, 8192 + kb * D + mb * 128:
                                 8192 + kb * D + (mb + 1) * 128],
                              Ab[:, rc:rc + n], start=st, stop=sp_),
                          reads=wk + Ctx.rkeys("A", rc, n, 2), writes=(psk,))
                d0 = 8192 + mb * LPC + o
                p.add("act", lambda pr, d0=d0, mb=mb, n=n, ps=ps:
                      pr.activation(out=A[:, d0:d0 + n], in_=ps[:, 0:n],
                                    func=AF.Identity, bias=bo[:, mb:mb + 1]),
                      reads=(psk, "bo"), writes=Ctx.rkeys("A", 2 * d0, 2 * n, 2))

        def ob_bf(i, b=b):
            d0 = (i * B + b) * D
            src = Ctx.rkeys("A", 2 * (8192 + i * LPC), 2 * LPC, 2)
            return Cb[:, d0:d0 + D], Ctx.rkeys("C", d0, D, 2)

        for i in range(DB):
            d0 = (i * B + b) * D
            p.add("dve", lambda pr, i=i, d0=d0: pr.tensor_copy(
                out=Cb[:, d0:d0 + D], in_=A[:, 8192 + i * LPC:8192 + (i + 1) * LPC]),
                reads=Ctx.rkeys("A", 8192 + i * LPC, LPC, 4),
                writes=Ctx.rkeys("C", d0, D, 2))
        r0 = 2 * b
        ln_stats(ctx, f"lnf{b}", ob_bf, DB, LPC, r0, LN_EPS, True, 0, 1,
                 sqt, sqk)

        for i in range(DB):
            for ci, (o, n) in enumerate(chunks(LPC, 512)):
                ak = Ctx.rkeys("A", 8192 + i * LPC + o, n, 4)
                sl = ci % 2
                p.dma("sp", xrl[:, sl * 512:sl * 512 + n],
                      g["xs"][:, i, b * LPC + o:b * LPC + o + n],
                      writes=(f"xrl{sl}",), tag="xreload")
                bcast_rows(ctx, r0, 1, o, n, rs_bc, ("l1a2rs",))
                bcast_rows(ctx, r0, 0, o, n, mr_bc, ("l1a2mr",))
                p.add("dve", lambda pr, i=i, o=o, n=n: pr.tensor_mul(
                    A[:, 8192 + i * LPC + o:8192 + i * LPC + o + n],
                    A[:, 8192 + i * LPC + o:8192 + i * LPC + o + n],
                    rs_bc[:, 0:n]), reads=ak + ("l1a2rs",), writes=ak)
                p.add("dve", lambda pr, i=i, o=o, n=n: pr.tensor_sub(
                    A[:, 8192 + i * LPC + o:8192 + i * LPC + o + n],
                    A[:, 8192 + i * LPC + o:8192 + i * LPC + o + n],
                    mr_bc[:, 0:n]), reads=ak + ("l1a2mr",), writes=ak)
                p.add("dve", lambda pr, i=i, o=o, n=n, sl=sl: pr.tensor_add(
                    A[:, 8192 + i * LPC + o:8192 + i * LPC + o + n],
                    A[:, 8192 + i * LPC + o:8192 + i * LPC + o + n],
                    xrl[:, sl * 512:sl * 512 + n]),
                    reads=ak + (f"xrl{sl}",), writes=ak)
                p.dma("act", xso[:, i, b * LPC + o:b * LPC + o + n],
                      A[:, 8192 + i * LPC + o:8192 + i * LPC + o + n],
                      reads=ak, writes=(f"xso{b}_{i}_{o}",), tag="xso",
                      mark=True)
    return g


def run_l1a(xs_cores, W):
    nc = new_launch()
    ctx = Ctx(nc)
    build_l1a1(nc, ctx)
    in_maps = [{"xs": xs_cores[c], "w_qkvT": W["w_qkvT"], "b_qkv": W["b_qkv"]}
               for c in range(NCORES)]
    res = run_launch(nc, ctx, in_maps)
    gstk = np.ascontiguousarray(
        np.stack([res.results[c]["gout"] for c in range(NCORES)]))
    qds = [res.results[c]["qd"] for c in range(NCORES)]

    nc2 = new_launch()
    ctx2 = Ctx(nc2)
    build_l1a2(nc2, ctx2)
    in2 = [{"xs": xs_cores[c], "gstk": gstk, "qd": qds[c], "w_oT": W["w_oT"],
            "b_o": W["b_o"], "headmask": W["headmask"]}
           for c in range(NCORES)]
    res2 = run_launch(nc2, ctx2, in2)
    return [r["xso"] for r in res2.results]


def build_l1f(nc, ctx):
    p = ctx.prog
    g = {}
    g["xs"] = nc.dram_tensor("xs", [128, DB, T], F32, kind="ExternalInput").ap()
    for nm, shp in (("qkv_wT", [D, 1536]), ("proj_wT", [D, D]),
                    ("router_wT", [D, 8]), ("ln1_w", [128, DB]),
                    ("ln1_b", [128, DB]), ("ln2_w", [128, DB]),
                    ("ln2_b", [128, DB]), ("moe_w1", [NEXP, D, FF]),
                    ("moe_w2", [NEXP, FF, D])):
        g[nm] = nc.dram_tensor(nm, shp, F32, kind="ExternalInput").ap()
    xso = nc.dram_tensor("xso", [128, DB, T], F32, kind="ExternalOutput").ap()

    A, Ab, Bf, Bb, Cb = ctx.A, ctx.Ab, ctx.Bp, ctx.Bb, ctx.Cb
    tmp = nc.alloc_sbuf_tensor("l1f_tmp", [128, 512], F32)
    rs_bc = nc.alloc_sbuf_tensor("l1f_rs", [128, 512], F32)
    mr_bc = nc.alloc_sbuf_tensor("l1f_mr", [128, 512], F32)
    xrl = nc.alloc_sbuf_tensor("l1f_xrl", [128, 1024], F32)
    w1c = nc.alloc_sbuf_tensor("l1f_w1c", [128, DB], F32)
    b1c = nc.alloc_sbuf_tensor("l1f_b1c", [128, DB], F32)
    w2c = nc.alloc_sbuf_tensor("l1f_w2c", [128, DB], F32)
    b2c = nc.alloc_sbuf_tensor("l1f_b2c", [128, DB], F32)
    rwt = nc.alloc_sbuf_tensor("l1f_rwt", [128, 64], BF16)
    kd = nc.dram_tensor("l1f_kd", [256, T], BF16, kind="Internal")
    vd = nc.dram_tensor("l1f_vd", [256, T], F32, kind="Internal")
    pd = nc.dram_tensor("l1f_pd", [32, LPC], F32, kind="Internal")
    wmd = nc.dram_tensor("l1f_wmd", [8, T], F32, kind="Internal")
    idbf = nc.alloc_sbuf_tensor("l1f_idbf", [128, 128], BF16)
    idf8 = nc.alloc_sbuf_tensor("l1f_idf8", [8, 8], F32)
    idf128 = nc.alloc_sbuf_tensor("l1f_idf", [128, 128], F32)
    colsel = nc.alloc_sbuf_tensor("l1f_csel", [128, 128], BF16)
    csel_np = np.zeros((128, 8, 16), dtype=np.float32)
    for m in range(8):
        for pp in range(128):
            csel_np[pp, m, 2 * m + pp // 64] = 1.0
    g_cs = nc.inline_tensor(
        csel_np.reshape(128, 128).astype(ml_dtypes.bfloat16), name="cselc").ap()
    g_id8 = nc.inline_tensor(np.eye(8, dtype=np.float32), name="id8c").ap()
    g_idf = nc.inline_tensor(np.eye(128, dtype=np.float32), name="idfc").ap()
    g_idb = nc.inline_tensor(
        np.eye(128, dtype=np.float32).astype(ml_dtypes.bfloat16),
        name="idbc").ap()
    p.dma("sp", colsel[:, :], g_cs, writes=("csel",))
    p.dma("sp", idf8[:, :], g_id8, writes=("id8",))
    p.dma("sp", idf128[:, :], g_idf, writes=("idf",))
    p.dma("sp", idbf[:, :], g_idb, writes=("idb",))
    for nm, t in (("ln1_w", w1c), ("ln1_b", b1c), ("ln2_w", w2c),
                  ("ln2_b", b2c)):
        p.dma("sp", t[:, :], g[nm], writes=(nm,))

    for i in range(DB):
        p.dma("sp", A[:, i * T:(i + 1) * T], g["xs"][:, i, :],
              writes=Ctx.rkeys("A", i * T, T, 4), tag="xload")
        p.add("dve", lambda pr, i=i: pr.tensor_copy(
            out=Cb[:, i * T:(i + 1) * T], in_=A[:, i * T:(i + 1) * T]),
            reads=Ctx.rkeys("A", i * T, T, 4),
            writes=Ctx.rkeys("C", i * T, T, 2))

    def xa_bf(i):
        return Cb[:, i * T:(i + 1) * T], Ctx.rkeys("C", i * T, T, 2)

    sqt = Cb[:, DB * T:DB * T + T]
    sq_keys = Ctx.rkeys("C", DB * T, T, 2)
    ln_stats(ctx, "lng", xa_bf, DB, T, 0, LN_EPS, True, 0, 1, sqt, sq_keys)
    for i in range(DB):
        ak = Ctx.rkeys("A", i * T, T, 4)
        for (o, n) in chunks(T, 512):
            bcast_rows(ctx, 0, 1, o, n, rs_bc, ("l1frs",))
            bcast_rows(ctx, 0, 0, o, n, mr_bc, ("l1fmr",))
            p.add("dve", lambda pr, i=i, o=o, n=n: pr.tensor_mul(
                tmp[:, 0:n], A[:, i * T + o:i * T + o + n], rs_bc[:, 0:n]),
                reads=ak + ("l1frs",), writes=("l1f_tmp",))
            p.add("dve", lambda pr, i=i, o=o, n=n: pr.tensor_sub(
                A[:, i * T + o:i * T + o + n], tmp[:, 0:n], mr_bc[:, 0:n]),
                reads=("l1f_tmp", "l1fmr"), writes=ak)

    for i in range(DB):
        p.add("dve", lambda pr, i=i: pr.tensor_copy(
            out=Cb[:, i * T:(i + 1) * T], in_=A[:, i * T:(i + 1) * T]),
            reads=Ctx.rkeys("A", i * T, T, 4),
            writes=Ctx.rkeys("C", i * T, T, 2))
    ln_stats(ctx, "ln1", xa_bf, DB, T, 4, LN_EPS, True, 0, 1, sqt, sq_keys)
    for i in range(DB):
        for (o, n) in chunks(T, 512):
            bcast_rows(ctx, 4, 1, o, n, rs_bc, ("l1frs",))
            bcast_rows(ctx, 4, 0, o, n, mr_bc, ("l1fmr",))
            p.add("dve", lambda pr, i=i, o=o, n=n: pr.tensor_mul(
                tmp[:, 0:n], A[:, i * T + o:i * T + o + n], rs_bc[:, 0:n]),
                reads=Ctx.rkeys("A", i * T, T, 4) + ("l1frs",),
                writes=("l1f_tmp",))
            p.add("dve", lambda pr, i=i, o=o, n=n: pr.tensor_sub(
                tmp[:, 0:n], tmp[:, 0:n], mr_bc[:, 0:n]),
                reads=("l1f_tmp", "l1fmr"), writes=("l1f_tmp",))
            p.add("dve", lambda pr, i=i, o=o, n=n: pr.tensor_scalar(
                out=Cb[:, i * T + o:i * T + o + n], in0=tmp[:, 0:n],
                scalar1=w1c[:, i:i + 1], scalar2=b1c[:, i:i + 1],
                op0=ALU.mult, op1=ALU.add),
                reads=("l1f_tmp", "ln1_w", "ln1_b"),
                writes=Ctx.rkeys("C", i * T + o, n, 2))

    wkeys = Ctx.rkeys("B", 0, 12288, 2)
    for kb in range(DB):
        load_w_bf(ctx, g["qkv_wT"][kb * 128:(kb + 1) * 128, :], 128, 1536,
                  Bb, kb * 1536, wkeys, q="act", stg_half=kb % 2)
    KOFF = 12288
    VOFF = 8192
    for mb in range(12):
        for ci, (o, n) in enumerate(chunks(T, 512)):
            pi = 4 + ((mb * 4 + ci) % 2)
            ps, psk = ctx.PS[pi], f"ps{pi}"
            for kb in range(DB):
                st, sp_ = (kb == 0), (kb == DB - 1)
                p.add("pe", lambda pr, kb=kb, mb=mb, o=o, n=n, st=st, sp_=sp_,
                      ps=ps: pr.matmul(
                          ps[:, 0:n],
                          Bb[:, kb * 1536 + mb * 128:kb * 1536 + (mb + 1) * 128],
                          Cb[:, kb * T + o:kb * T + o + n],
                          start=st, stop=sp_),
                      reads=wkeys + Ctx.rkeys("C", kb * T + o, n, 2),
                      writes=(psk,))
            if mb < 8:
                d0 = 16384 + mb * T + o
                p.add("act", lambda pr, d0=d0, n=n, ps=ps: pr.copy(
                    out=Cb[:, d0:d0 + n], in_=ps[:, 0:n]),
                    reads=(psk,), writes=Ctx.rkeys("C", d0, n, 2))
            elif mb < 10:
                d0 = KOFF + (mb - 8) * T + o
                p.add("act", lambda pr, d0=d0, n=n, ps=ps: pr.copy(
                    out=Bb[:, d0:d0 + n], in_=ps[:, 0:n]),
                    reads=(psk,), writes=Ctx.rkeys("B", d0, n, 2))
            else:
                d0 = VOFF + (mb - 10) * T + o
                p.add("act", lambda pr, d0=d0, n=n, ps=ps: pr.copy(
                    out=Bf[:, d0:d0 + n], in_=ps[:, 0:n]),
                    reads=(psk,), writes=Ctx.rkeys("B", d0, n, 4))
    for mb in range(2):
        p.dma("act", kd.ap()[mb * 128:(mb + 1) * 128, :],
              Bb[:, KOFF + mb * T:KOFF + (mb + 1) * T],
              reads=Ctx.rkeys("B", KOFF + mb * T, T, 2), writes=("kd",))
        p.dma("act", vd.ap()[mb * 128:(mb + 1) * 128, :],
              Bf[:, VOFF + mb * T:VOFF + (mb + 1) * T],
              reads=Ctx.rkeys("B", VOFF + mb * T, T, 4), writes=("vd",))

    QK0, QK1, KRE = 0, 1024, 2048
    V0R, V1R, P0R, P1R = 2048, 3072, 4096, 5120
    qkk0 = Ctx.rkeys("B", QK0, 1024, 2)
    qkk1 = Ctx.rkeys("B", QK1, 1024, 2)
    krk = Ctx.rkeys("B", KRE, 2048, 2)
    vrk = (Ctx.rkeys("B", V0R, 1024, 4) + Ctx.rkeys("B", V1R, 1024, 4) +
           Ctx.rkeys("B", P0R, 1024, 4) + Ctx.rkeys("B", P1R, 1024, 4))
    for m in range(8):
        j = m // 2
        p.dma("sp", Bb[0:64, KRE:KRE + T],
              kd.ap()[64 * j:64 * j + 64, :], reads=("kd",), writes=krk)
        p.dma("sp", Bb[64:128, KRE:KRE + T], kd.ap()[64 * j:64 * j + 64, :],
              reads=("kd",), writes=krk)
        q1 = 16384 + m * T + LPC
        p.add("dve", lambda pr, m=m, q1=q1: pr.tensor_mul(
            Bb[:, QK0:QK0 + LPC], Cb[:, q1:q1 + LPC], Bb[:, KRE:KRE + LPC]),
            reads=Ctx.rkeys("C", q1, LPC, 2) + krk, writes=qkk0)
        p.add("dve", lambda pr, m=m, q1=q1: pr.tensor_mul(
            Bb[:, QK1:QK1 + LPC], Cb[:, q1:q1 + LPC],
            Bb[:, KRE + LPC:KRE + T]),
            reads=Ctx.rkeys("C", q1, LPC, 2) + krk, writes=qkk1)
        for ci, (o, n) in enumerate(chunks(LPC, 512)):
            st, sp_ = (m == 0), (m == 7)
            p.add("pe", lambda pr, m=m, o=o, n=n, st=st, sp_=sp_, ci=ci:
                  pr.matmul(ctx.PS[ci][0:16, 0:n], colsel[:, m * 16:(m + 1) * 16],
                            Bb[:, QK0 + o:QK0 + o + n], start=st, stop=sp_),
                  reads=qkk0 + ("csel",), writes=(f"ps{ci}",))
            p.add("pe", lambda pr, m=m, o=o, n=n, st=st, sp_=sp_, ci=ci:
                  pr.matmul(ctx.PS[2 + ci][0:16, 0:n],
                            colsel[:, m * 16:(m + 1) * 16],
                            Bb[:, QK1 + o:QK1 + o + n], start=st, stop=sp_),
                  reads=qkk1 + ("csel",), writes=(f"ps{2 + ci}",))

    S0, S1, SM = 6144, 7168, 8192
    s10 = Bf[0:16, S0:S0 + LPC]
    s11 = Bf[0:16, S1:S1 + LPC]
    smx = Bf[0:16, SM:SM + LPC]
    sk = Ctx.rkeys("B", S0, 3104, 4)
    p.add("dve", lambda pr: pr.memset(ctx.cst[:, 1:2], 0.125), writes=("cst8",))
    for ci, (o, n) in enumerate(chunks(LPC, 512)):
        p.add("dve", lambda pr, o=o, n=n, ci=ci: pr.tensor_copy(
            out=s10[:, o:o + n], in_=ctx.PS[ci][0:16, 0:n]),
            reads=(f"ps{ci}",), writes=sk)
        p.add("dve", lambda pr, o=o, n=n, ci=ci: pr.tensor_copy(
            out=s11[:, o:o + n], in_=ctx.PS[2 + ci][0:16, 0:n]),
            reads=(f"ps{2 + ci}",), writes=sk)
    p.add("dve", lambda pr: pr.tensor_max(smx[:, :], s10[:, :], s11[:, :]),
          reads=sk, writes=sk)
    p.add("dve", lambda pr: pr.tensor_sub(s10[:, :], s10[:, :], smx[:, :]),
          reads=sk, writes=sk)
    p.add("dve", lambda pr: pr.tensor_sub(s11[:, :], s11[:, :], smx[:, :]),
          reads=sk, writes=sk)
    p.add("act", lambda pr: pr.activation(out=s10[:, :], in_=s10[:, :],
                                          func=AF.Exp,
                                          scale=ctx.cst[0:16, 1:2]),
          sync_reads=sk + ("cst8",), writes=sk)
    p.add("act", lambda pr: pr.activation(out=s11[:, :], in_=s11[:, :],
                                          func=AF.Exp,
                                          scale=ctx.cst[0:16, 1:2]),
          reads=("cst8",), sync_reads=sk, writes=sk)
    p.add("dve", lambda pr: pr.tensor_add(smx[:, :], s10[:, :], s11[:, :]),
          sync_reads=sk, writes=sk)
    p.add("dve", lambda pr: pr.reciprocal(smx[:, :], smx[:, :]),
          sync_reads=sk, writes=sk)
    p.add("dve", lambda pr: pr.tensor_mul(s10[:, :], s10[:, :], smx[:, :]),
          sync_reads=sk, writes=sk)
    p.add("dve", lambda pr: pr.tensor_mul(s11[:, :], s11[:, :], smx[:, :]),
          sync_reads=sk, writes=sk)
    p.dma("sp", pd.ap()[0:16, :], s10[:, :], reads=sk, writes=("pd",))
    p.dma("sp", pd.ap()[16:32, :], s11[:, :], reads=sk, writes=("pd",))

    vk = vrk
    for m in range(8):
        j = m // 2
        p.dma("sp", Bf[0:64, V0R:V0R + LPC], vd.ap()[64 * j:64 * j + 64, 0:LPC],
              reads=("vd",), writes=vk)
        p.dma("sp", Bf[64:128, V0R:V0R + LPC], vd.ap()[64 * j:64 * j + 64, 0:LPC],
              reads=("vd",), writes=vk)
        p.dma("sp", Bf[0:64, V1R:V1R + LPC],
              vd.ap()[64 * j:64 * j + 64, LPC:T], reads=("vd",), writes=vk)
        p.dma("sp", Bf[64:128, V1R:V1R + LPC],
              vd.ap()[64 * j:64 * j + 64, LPC:T], reads=("vd",), writes=vk)
        for (r0, row) in ((P0R, 2 * m), (P1R, 16 + 2 * m)):
            p.dma("sp", Bf[0:64, r0:r0 + LPC],
                  pd.ap()[row:row + 1, :].to_broadcast([64, LPC]),
                  reads=("pd",), writes=vk)
            p.dma("sp", Bf[64:128, r0:r0 + LPC],
                  pd.ap()[row + 1:row + 2, :].to_broadcast([64, LPC]),
                  reads=("pd",), writes=vk)
        a0 = m * T
        p.add("dve", lambda pr, a0=a0: pr.tensor_copy(
            out=Cb[:, a0:a0 + LPC], in_=Bf[:, V0R:V0R + LPC]),
            reads=vk, writes=Ctx.rkeys("C", a0, LPC, 2))
        for (o, n) in chunks(LPC, 512):
            p.add("dve", lambda pr, o=o, n=n: pr.tensor_mul(
                tmp[:, 0:n], Bf[:, P0R + o:P0R + o + n],
                Bf[:, V0R + o:V0R + o + n]), reads=vk, writes=("l1f_tmp",))
            p.add("dve", lambda pr, o=o, n=n: pr.tensor_mul(
                xrl[:, 0:n], Bf[:, P1R + o:P1R + o + n],
                Bf[:, V1R + o:V1R + o + n]), reads=vk, writes=("l1f_xrl",))
            p.add("dve", lambda pr, a0=a0, o=o, n=n: pr.tensor_add(
                Cb[:, a0 + LPC + o:a0 + LPC + o + n], tmp[:, 0:n],
                xrl[:, 0:n]), reads=("l1f_tmp", "l1f_xrl"),
                writes=Ctx.rkeys("C", a0 + LPC + o, n, 2))

    pk = Ctx.rkeys("B", 16384, 8192, 2)
    for kb in range(DB):
        load_w_bf(ctx, g["proj_wT"][kb * 128:(kb + 1) * 128, :], 128, D,
                  Bb, 16384 + kb * D, pk, q="act", stg_half=kb % 2)
    for mb in range(DB):
        for ci, (o, n) in enumerate(chunks(T, 512)):
            pi = 4 + ((mb * 4 + ci) % 2)
            ps, psk = ctx.PS[pi], f"ps{pi}"
            for kb in range(DB):
                st, sp_ = (kb == 0), (kb == DB - 1)
                p.add("pe", lambda pr, kb=kb, mb=mb, o=o, n=n, st=st, sp_=sp_,
                      ps=ps: pr.matmul(
                          ps[:, 0:n],
                          Bb[:, 16384 + kb * D + mb * 128:
                             16384 + kb * D + (mb + 1) * 128],
                          Cb[:, kb * T + o:kb * T + o + n],
                          start=st, stop=sp_),
                      reads=pk + Ctx.rkeys("C", kb * T + o, n, 2),
                      writes=(psk,))
            ak = Ctx.rkeys("A", mb * T + o, n, 4)
            p.add("dve", lambda pr, mb=mb, o=o, n=n, ps=ps: pr.tensor_add(
                A[:, mb * T + o:mb * T + o + n],
                A[:, mb * T + o:mb * T + o + n], ps[:, 0:n]),
                reads=(psk,) + ak, writes=ak)

    def ha_bf(i):
        d0 = 16384 + i * T
        return Cb[:, d0:d0 + T], Ctx.rkeys("C", d0, T, 2)

    for i in range(DB):
        d0 = 16384 + i * T
        p.add("dve", lambda pr, i=i, d0=d0: pr.tensor_copy(
            out=Cb[:, d0:d0 + T], in_=A[:, i * T:(i + 1) * T]),
            reads=Ctx.rkeys("A", i * T, T, 4),
            writes=Ctx.rkeys("C", d0, T, 2))
    sqt2 = Bb[:, KOFF:KOFF + T]
    sqk2 = Ctx.rkeys("B", KOFF, T, 2)
    ln_stats(ctx, "ln2", ha_bf, DB, T, 2, LN_EPS, True, 0, 1, sqt2, sqk2)
    for i in range(DB):
        d0 = 16384 + i * T
        for (o, n) in chunks(T, 512):
            bcast_rows(ctx, 2, 1, o, n, rs_bc, ("l1frs",))
            bcast_rows(ctx, 2, 0, o, n, mr_bc, ("l1fmr",))
            p.add("dve", lambda pr, i=i, o=o, n=n: pr.tensor_mul(
                tmp[:, 0:n], A[:, i * T + o:i * T + o + n], rs_bc[:, 0:n]),
                reads=Ctx.rkeys("A", i * T, T, 4) + ("l1frs",),
                writes=("l1f_tmp",))
            p.add("dve", lambda pr, o=o, n=n: pr.tensor_sub(
                tmp[:, 0:n], tmp[:, 0:n], mr_bc[:, 0:n]),
                reads=("l1f_tmp", "l1fmr"), writes=("l1f_tmp",))
            p.add("dve", lambda pr, i=i, d0=d0, o=o, n=n: pr.tensor_scalar(
                out=Cb[:, d0 + o:d0 + o + n], in0=tmp[:, 0:n],
                scalar1=w2c[:, i:i + 1], scalar2=b2c[:, i:i + 1],
                op0=ALU.mult, op1=ALU.add),
                reads=("l1f_tmp", "ln2_w", "ln2_b"),
                writes=Ctx.rkeys("C", d0 + o, n, 2))

    for kb in range(DB):
        load_w_bf(ctx, g["router_wT"][kb * 128:(kb + 1) * 128, :], 128, 8,
                  rwt, kb * 8, ("rwt",), q="act", stg_half=kb % 2)
    LG = 9216
    lgk = Ctx.rkeys("B", LG, 2048, 4)
    for ci, (o, n) in enumerate(chunks(T, 512)):
        ps, psk = ctx.PS[4 + ci % 2], f"ps{4 + ci % 2}"
        for kb in range(DB):
            st, sp_ = (kb == 0), (kb == DB - 1)
            p.add("pe", lambda pr, kb=kb, o=o, n=n, st=st, sp_=sp_, ps=ps:
                  pr.matmul(ps[0:8, 0:n], rwt[:, kb * 8:(kb + 1) * 8],
                            Cb[:, 16384 + kb * T + o:16384 + kb * T + o + n],
                            start=st, stop=sp_),
                  reads=("rwt",) + Ctx.rkeys("C", 16384 + kb * T + o, n, 2),
                  writes=(psk,))
        p.add("dve", lambda pr, o=o, n=n, ps=ps: pr.tensor_copy(
            out=Bf[0:8, LG + o:LG + o + n], in_=ps[0:8, 0:n]),
            reads=(psk,), writes=lgk)
    WMT = 6144
    wmk = Ctx.rkeys("B", WMT, 2048, 4)
    lgt = Bf[0:128, 11264:11264 + 8]
    ltk = Ctx.rkeys("B", 11264, 40, 4)
    top8 = Bf[0:128, 11272:11272 + 8]
    sel8 = Bf[0:128, 11280:11280 + 8]
    for t16 in range(16):
        c0 = t16 * 128
        p.add("pe", lambda pr, c0=c0: pr.transpose(
            ctx.PS[0][:, 0:8], Bf[0:8, LG + c0:LG + c0 + 128], idf8[:, :]),
            reads=lgk + ("id8",), writes=("ps0",))
        p.add("dve", lambda pr: pr.tensor_copy(out=lgt[:, :],
                                               in_=ctx.PS[0][:, 0:8]),
              reads=("ps0",), writes=ltk)
        p.add("dve", lambda pr: pr.reduce_max(
            out=Bf[0:128, 11288:11289], in_=lgt[:, :], axis=AX),
            sync_reads=ltk, writes=ltk)
        p.add("dve", lambda pr: pr.tensor_scalar_mul(
            Bf[0:128, 11289:11290], Bf[0:128, 11288:11289], -1.0),
            sync_reads=ltk, writes=ltk)
        p.add("act", lambda pr: pr.activation(
            out=lgt[:, :], in_=lgt[:, :], func=AF.Exp,
            bias=Bf[0:128, 11289:11290]),
            sync_reads=ltk, writes=ltk)
        p.add("dve", lambda pr: pr.reduce_sum(
            out=Bf[0:128, 11288:11289], in_=lgt[:, :], axis=AX),
            sync_reads=ltk, writes=ltk)
        p.add("dve", lambda pr: pr.reciprocal(
            Bf[0:128, 11289:11290], Bf[0:128, 11288:11289]),
            sync_reads=ltk, writes=ltk)
        p.add("dve", lambda pr: pr.tensor_scalar_mul(
            lgt[:, :], lgt[:, :], Bf[0:128, 11289:11290]),
            sync_reads=ltk, writes=ltk)
        p.add("dve", lambda pr: pr.max(out=top8[:, :], in_=lgt[:, :]),
              sync_reads=ltk, writes=ltk)
        p.add("dve", lambda pr: pr.tensor_scalar(
            out=sel8[:, :], in0=lgt[:, :], scalar1=top8[:, 1:2], scalar2=None,
            op0=ALU.is_ge), sync_reads=ltk, writes=ltk)
        p.add("dve", lambda pr: pr.tensor_mul(lgt[:, :], lgt[:, :],
                                              sel8[:, :]),
              sync_reads=ltk, writes=ltk)
        p.add("pe", lambda pr: pr.transpose(
            ctx.PS[1][0:8, 0:128], lgt[:, :], idf128[:, :]),
            sync_reads=ltk, reads=("idf",), writes=("ps1",))
        p.add("dve", lambda pr, c0=c0: pr.tensor_copy(
            out=Bf[0:8, WMT + c0:WMT + c0 + 128], in_=ctx.PS[1][0:8, 0:128]),
            reads=("ps1",), writes=wmk)
    p.dma("sp", wmd.ap()[:, :], Bf[0:8, WMT:WMT + T], reads=wmk,
          writes=("wmd",))

    wm_bc = nc.alloc_sbuf_tensor("l1f_wmbc", [128, 512], F32)
    gsc = nc.alloc_sbuf_tensor("l1f_gsc", [128, 512], F32)
    for e in range(NEXP):
        for half in range(2):
            w1k = Ctx.rkeys("B", 0, 8192, 2)
            w2k = Ctx.rkeys("B", 8192, 8192, 2)
            for kb in range(DB):
                load_w_bf(ctx, g["moe_w1"][e, kb * 128:(kb + 1) * 128,
                                           half * 1024:(half + 1) * 1024],
                          128, 1024, Bb, kb * 1024, w1k, q="act",
                          stg_half=kb % 2)
            for fb in range(DB):
                load_w_bf(ctx, g["moe_w2"][e, half * 1024 + fb * 128:
                                           half * 1024 + (fb + 1) * 128, :],
                          128, D, Bb, 8192 + fb * D, w2k, q="act",
                          stg_half=fb % 2)
            for (o, n) in chunks(T, 512):
                p.dma("sp", wm_bc[:, 0:n],
                      wmd.ap()[e:e + 1, o:o + n].to_broadcast([128, n]),
                      reads=("wmd",), writes=("wmbc",))
                hk = Ctx.rkeys("C", 0, 4096, 2)
                for fb in range(DB):
                    ps, psk = ctx.PS[4 + fb % 2], f"ps{4 + fb % 2}"
                    for kb in range(DB):
                        st, sp_ = (kb == 0), (kb == DB - 1)
                        p.add("pe", lambda pr, kb=kb, fb=fb, o=o, n=n, st=st,
                              sp_=sp_, ps=ps: pr.matmul(
                                  ps[:, 0:n],
                                  Bb[:, kb * 1024 + fb * 128:
                                     kb * 1024 + (fb + 1) * 128],
                                  Cb[:, 16384 + kb * T + o:
                                     16384 + kb * T + o + n],
                                  start=st, stop=sp_),
                              reads=w1k +
                              Ctx.rkeys("C", 16384 + kb * T + o, n, 2),
                              writes=(psk,))
                    p.add("act", lambda pr, fb=fb, n=n, ps=ps: pr.activation(
                        out=gsc[:, 0:n], in_=ps[:, 0:n],
                        func=AF.Gelu_apprx_tanh),
                        reads=(psk,), writes=("l1f_gsc",))
                    p.add("dve", lambda pr, fb=fb, n=n: pr.tensor_mul(
                        Cb[:, fb * 512:fb * 512 + n], gsc[:, 0:n],
                        wm_bc[:, 0:n]),
                        reads=("l1f_gsc", "wmbc"),
                        writes=Ctx.rkeys("C", fb * 512, 512, 2))
                for db in range(DB):
                    ps, psk = ctx.PS[4 + db % 2], f"ps{4 + db % 2}"
                    for fb in range(DB):
                        st, sp_ = (fb == 0), (fb == DB - 1)
                        p.add("pe", lambda pr, db=db, fb=fb, n=n, st=st,
                              sp_=sp_, ps=ps: pr.matmul(
                                  ps[:, 0:n],
                                  Bb[:, 8192 + fb * D + db * 128:
                                     8192 + fb * D + (db + 1) * 128],
                                  Cb[:, fb * 512:fb * 512 + n],
                                  start=st, stop=sp_),
                              reads=w2k + hk, writes=(psk,))
                    ak = Ctx.rkeys("A", db * T + o, n, 4)
                    p.add("dve", lambda pr, db=db, o=o, n=n, ps=ps:
                          pr.tensor_add(A[:, db * T + o:db * T + o + n],
                                        A[:, db * T + o:db * T + o + n],
                                        ps[:, 0:n]),
                          reads=(psk,) + ak, writes=ak)

    def haf_bf(i):
        return Cb[:, i * T:(i + 1) * T], Ctx.rkeys("C", i * T, T, 2)

    for i in range(DB):
        p.add("dve", lambda pr, i=i: pr.tensor_copy(
            out=Cb[:, i * T:(i + 1) * T], in_=A[:, i * T:(i + 1) * T]),
            reads=Ctx.rkeys("A", i * T, T, 4),
            writes=Ctx.rkeys("C", i * T, T, 2))
    ln_stats(ctx, "lnh", haf_bf, DB, T, 6, LN_EPS, True, 0, 1, sqt2, sqk2)
    for i in range(DB):
        for ci, (o, n) in enumerate(chunks(T, 512)):
            ak = Ctx.rkeys("A", i * T + o, n, 4)
            sl = ci % 2
            p.dma("sp", xrl[:, sl * 512:sl * 512 + n], g["xs"][:, i, o:o + n],
                  writes=(f"l1fxrl{sl}",), tag="xreload")
            bcast_rows(ctx, 6, 1, o, n, rs_bc, ("l1frs",))
            bcast_rows(ctx, 6, 0, o, n, mr_bc, ("l1fmr",))
            p.add("dve", lambda pr, i=i, o=o, n=n: pr.tensor_mul(
                A[:, i * T + o:i * T + o + n], A[:, i * T + o:i * T + o + n],
                rs_bc[:, 0:n]), reads=ak + ("l1frs",), writes=ak)
            p.add("dve", lambda pr, i=i, o=o, n=n: pr.tensor_sub(
                A[:, i * T + o:i * T + o + n], A[:, i * T + o:i * T + o + n],
                mr_bc[:, 0:n]), reads=ak + ("l1fmr",), writes=ak)
            p.add("dve", lambda pr, i=i, o=o, n=n, sl=sl: pr.tensor_add(
                A[:, i * T + o:i * T + o + n], A[:, i * T + o:i * T + o + n],
                xrl[:, sl * 512:sl * 512 + n]),
                reads=ak + (f"l1fxrl{sl}",), writes=ak)
            p.dma("act", xso[:, i, o:o + n], A[:, i * T + o:i * T + o + n],
                  reads=ak, writes=(f"xso{i}_{o}",), tag="xso", mark=True)
    return g


def run_l1f(xs_cores, W):
    nc = new_launch()
    ctx = Ctx(nc)
    build_l1f(nc, ctx)
    in_maps = [{"xs": xs_cores[c], "qkv_wT": W["qkv_wT"],
                "proj_wT": W["proj_wT"], "router_wT": W["router_wT"],
                "ln1_w": W["ln1_w"], "ln1_b": W["ln1_b"],
                "ln2_w": W["ln2_w"], "ln2_b": W["ln2_b"],
                "moe_w1": W["moe_w1"], "moe_w2": W["moe_w2"]}
               for c in range(NCORES)]
    res = run_launch(nc, ctx, in_maps)
    return [r["xso"] for r in res.results]


def build_l0a1(nc, ctx):
    p = ctx.prog
    g = {}
    g["xsin"] = nc.dram_tensor("xsin", [128, DB, THT], F32,
                               kind="ExternalInput").ap()
    for nm, shp in (("rms1_w", [128, DB]), ("W_inT", [D, 2 * ED]),
                    ("conv_w", [128, EG * 4]), ("conv_b", [128, EG]),
                    ("W_xprojT", [ED, 96]), ("W_dtT", [RK, ED]),
                    ("b_dt", [128, EG]), ("A_log", [128, EG * NST])):
        g[nm] = nc.dram_tensor(nm, shp, F32, kind="ExternalInput").ap()
    st1 = nc.dram_tensor("st1", [128, 544], F32, kind="ExternalOutput").ap()
    xcd = nc.dram_tensor("xcd", [128, EG, T], F32, kind="ExternalOutput").ap()
    szd = nc.dram_tensor("szd", [128, EG, T], F32, kind="ExternalOutput").ap()
    dtd = nc.dram_tensor("dtd", [128, EG, T], F32, kind="ExternalOutput").ap()
    bcd = nc.dram_tensor("bcd", [32, T], F32, kind="ExternalOutput").ap()
    rowsa = nc.dram_tensor("rowsa", [2, THT], F32, kind="ExternalOutput").ap()

    A, Ab, Bf, Bb, Cb = ctx.A, ctx.Ab, ctx.Bp, ctx.Bb, ctx.Cb
    rms1 = nc.alloc_sbuf_tensor("l0_rms1", [128, DB], F32)
    cwt = nc.alloc_sbuf_tensor("l0_cw", [128, EG * 4], F32)
    cbt = nc.alloc_sbuf_tensor("l0_cb", [128, EG], F32)
    bdt = nc.alloc_sbuf_tensor("l0_bdt", [128, EG], F32)
    att = nc.alloc_sbuf_tensor("l0_at", [128, EG * NST], F32)
    wxp = nc.alloc_sbuf_tensor("l0_wxp", [128, EG * 96], BF16)
    wdt = nc.alloc_sbuf_tensor("l0_wdt", [64, ED], BF16)
    stt = nc.alloc_sbuf_tensor("l0_stt", [128, 544], F32)
    tmp = nc.alloc_sbuf_tensor("l0_tmp", [128, 512], F32)
    rs_bc = nc.alloc_sbuf_tensor("l0_rs", [128, 512], F32)
    mr_bc = nc.alloc_sbuf_tensor("l0_mr", [128, 512], F32)
    for nm, t in (("rms1_w", rms1), ("conv_w", cwt), ("conv_b", cbt),
                  ("b_dt", bdt)):
        p.dma("sp", t[:, :], g[nm], writes=(nm,))
    p.dma("sp", att[:, :], g["A_log"], writes=("Atl",))
    att2 = nc.alloc_sbuf_tensor("l0_at2", [128, EG * NST], F32)
    p.add("act", lambda pr: pr.activation(out=att2[:, :], in_=att[:, :],
                                          func=AF.Exp),
          reads=("Atl",), writes=("At2",))
    p.add("dve", lambda pr: pr.tensor_scalar_mul(att[:, :], att2[:, :], -1.0),
          reads=("At2",), writes=("At",))
    for gb in range(EG):
        load_w_bf(ctx, g["W_xprojT"][gb * 128:(gb + 1) * 128, :], 128, 96,
                  wxp, gb * 96, ("wxp",), q="act", stg_half=gb % 2)
    for _ in range(1):
        load_w_bf(ctx, g["W_dtT"][0:RK, :], RK, ED, wdt, 0, ("wdt",),
                  q="act", stg_half=0)

    def xslice(i):
        if i < 7:
            return A[:, i * THT:(i + 1) * THT], Ctx.rkeys("A", i * THT, THT, 4)
        return Bf[:, 0:THT], Ctx.rkeys("B", 0, THT, 4)

    for i in range(DB):
        xa, xk = xslice(i)
        p.dma("sp", xa, g["xsin"][:, i, :], writes=xk, tag="xload")
        p.add("dve", lambda pr, i=i, xa=xa: pr.tensor_copy(
            out=Cb[:, i * THT:(i + 1) * THT], in_=xa),
            reads=xk, writes=Ctx.rkeys("C", i * THT, THT, 2))

    def xa_bf(i):
        return (Cb[:, i * THT:(i + 1) * THT], Ctx.rkeys("C", i * THT, THT, 2))

    sqt = Cb[:, DB * THT:DB * THT + THT]
    sq_keys = Ctx.rkeys("C", DB * THT, THT, 2)
    ln_stats(ctx, "lna", xa_bf, DB, THT, 0, LN_EPS, True, 0, 1, sqt, sq_keys)
    p.dma("sp", rowsa[0:2, :], ctx.rows_dram.ap()[0:2, 0:THT],
          reads=("rows:0d",), writes=("rowsa",), mark=True)
    for i in range(DB):
        xa, xk = xslice(i)
        for (o, n) in chunks(THT, 512):
            bcast_rows(ctx, 0, 1, o, n, rs_bc, ("l0rs",))
            bcast_rows(ctx, 0, 0, o, n, mr_bc, ("l0mr",))
            p.add("dve", lambda pr, xa=xa, o=o, n=n: pr.tensor_mul(
                tmp[:, 0:n], xa[:, o:o + n], rs_bc[:, 0:n]),
                reads=xk + ("l0rs",), writes=("l0_tmp",))
            p.add("dve", lambda pr, o=o, n=n: pr.tensor_sub(
                tmp[:, 0:n], tmp[:, 0:n], mr_bc[:, 0:n]),
                reads=("l0_tmp", "l0mr"), writes=("l0_tmp",))
            p.add("dve", lambda pr, i=i, o=o, n=n: pr.tensor_scalar_mul(
                Cb[:, i * THT + o:i * THT + o + n], tmp[:, 0:n],
                rms1[:, i:i + 1]),
                reads=("l0_tmp", "rms1_w"),
                writes=Ctx.rkeys("C", i * THT + o, n, 2))

    wik = Ctx.rkeys("A", 0, 32768, 2)
    for kb in range(DB):
        load_w_bf(ctx, g["W_inT"][kb * 128:(kb + 1) * 128, :], 128, 2 * ED,
                  Ab, kb * 2 * ED, wik, q="act", stg_half=kb % 2)

    XM0 = 0
    XC0 = 2060
    XS1 = 8204
    XS2 = 5132
    XB0 = (4108 + 2054, )
    xmk = Ctx.rkeys("B", XM0, THT, 4)
    xck = Ctx.rkeys("B", XC0, T, 4)
    xsk = Ctx.rkeys("B", XS1, T, 4)
    xs2k = Ctx.rkeys("B", XS2, T, 4)
    ngrp = 0
    for gb in range(EG):
        for ci, (o, n) in enumerate(chunks(THT, 512)):
            pi = 4 + (ngrp % 2)
            ngrp += 1
            ps, psk = ctx.PS[pi], f"ps{pi}"
            for kb in range(DB):
                st, sp_ = (kb == 0), (kb == DB - 1)
                p.add("pe", lambda pr, kb=kb, gb=gb, o=o, n=n, st=st, sp_=sp_,
                      ps=ps: pr.matmul(
                          ps[:, 0:n],
                          Ab[:, kb * 2 * ED + gb * 128:kb * 2 * ED + (gb + 1) * 128],
                          Cb[:, kb * THT + o:kb * THT + o + n],
                          start=st, stop=sp_),
                      reads=wik + Ctx.rkeys("C", kb * THT + o, n, 2),
                      writes=(psk,))
            p.add("act", lambda pr, o=o, n=n, ps=ps: pr.copy(
                out=Bf[:, XM0 + o:XM0 + o + n], in_=ps[:, 0:n]),
                reads=(psk,), writes=xmk)
        import os
        _cm = int(os.environ.get("KCONVMODE", "0"))
        ACC2 = XC0 + T
        ack2 = Ctx.rkeys("B", ACC2, LPC, 4)
        for b in range(B):
            d0 = XC0 + b * LPC
            if _cm in (1, 3):
                p.add("dve", lambda pr, gb=gb, b=b, d0=d0: pr.tensor_copy(
                    out=Bf[:, d0:d0 + LPC],
                    in_=Bf[:, XM0 + b * TH + HALO:XM0 + b * TH + HALO + LPC]),
                    reads=xmk, writes=xck)
                continue
            p.add("dve", lambda pr, gb=gb, b=b: pr.tensor_scalar_mul(
                Bf[:, ACC2:ACC2 + LPC], Bf[:, XM0 + b * TH:XM0 + b * TH + LPC],
                cwt[:, gb * 4:gb * 4 + 1]),
                reads=xmk + ("conv_w",), writes=ack2)
            p.add("dve", lambda pr, gb=gb, b=b, d0=d0:
                  pr.scalar_tensor_tensor(
                      out=Bf[:, d0:d0 + LPC],
                      in0=Bf[:, XM0 + b * TH + 1:XM0 + b * TH + 1 + LPC],
                      scalar=cwt[:, gb * 4 + 1:gb * 4 + 2],
                      in1=Bf[:, ACC2:ACC2 + LPC], op0=ALU.mult, op1=ALU.add),
                  reads=xmk + ("conv_w",) + ack2, writes=xck)
            p.add("dve", lambda pr, gb=gb, b=b, d0=d0:
                  pr.scalar_tensor_tensor(
                      out=Bf[:, ACC2:ACC2 + LPC],
                      in0=Bf[:, XM0 + b * TH + 2:XM0 + b * TH + 2 + LPC],
                      scalar=cwt[:, gb * 4 + 2:gb * 4 + 3],
                      in1=Bf[:, d0:d0 + LPC], op0=ALU.mult, op1=ALU.add),
                  reads=xmk + ("conv_w",) + xck, writes=ack2)
            p.add("dve", lambda pr, gb=gb, b=b, d0=d0:
                  pr.scalar_tensor_tensor(
                      out=Bf[:, d0:d0 + LPC],
                      in0=Bf[:, XM0 + b * TH + 3:XM0 + b * TH + 3 + LPC],
                      scalar=cwt[:, gb * 4 + 3:gb * 4 + 4],
                      in1=Bf[:, ACC2:ACC2 + LPC], op0=ALU.mult, op1=ALU.add),
                  reads=xmk + ("conv_w",) + ack2, writes=xck)
        if _cm in (2, 3):
            pass
        else:
            p.add("dve", lambda pr, gb=gb: pr.tensor_scalar_add(
                Bf[:, XS1:XS1 + T], Bf[:, XC0:XC0 + T], cbt[:, gb:gb + 1]),
                reads=xck + ("conv_b",), writes=xsk)
            p.add("act", lambda pr, gb=gb: pr.activation(
                out=Bf[:, XC0:XC0 + T], in_=Bf[:, XS1:XS1 + T], func=AF.Silu),
                reads=xsk, writes=xck)
        xs_src, xs_srck = (Bf[:, XC0:XC0 + T], xck)
        p.dma("act", xcd[:, gb, :], xs_src, reads=xs_srck,
              writes=(f"xcd{gb}",), mark=True)
        xcbk = Ctx.rkeys("C", DB * THT, T, 2)
        p.add("dve", lambda pr, gb=gb, xs_src=xs_src: pr.tensor_copy(
            out=Cb[:, DB * THT:DB * THT + T], in_=xs_src),
            reads=xs_srck, writes=xcbk)
        for ci, (o, n) in enumerate(chunks(T, 512)):
            st, sp_ = (gb == 0), (gb == EG - 1)
            p.add("pe", lambda pr, gb=gb, o=o, n=n, st=st, sp_=sp_, ci=ci:
                  pr.matmul(ctx.PS[ci][0:96, 0:n],
                            wxp[:, gb * 96:(gb + 1) * 96],
                            Cb[:, DB * THT + o:DB * THT + o + n],
                            start=st, stop=sp_),
                  reads=xcbk + ("wxp",), writes=(f"ps{ci}",))

    for gz in range(EG):
        for b in range(B):
            for ci, (o, n) in enumerate(chunks(LPC, 512)):
                pi = 4 + (ngrp % 2)
                ngrp += 1
                ps, psk = ctx.PS[pi], f"ps{pi}"
                ro = b * TH + HALO + o
                for kb in range(DB):
                    st, sp_ = (kb == 0), (kb == DB - 1)
                    p.add("pe", lambda pr, kb=kb, gz=gz, ro=ro, n=n, st=st,
                          sp_=sp_, ps=ps: pr.matmul(
                              ps[:, 0:n],
                              Ab[:, kb * 2 * ED + ED + gz * 128:
                                 kb * 2 * ED + ED + (gz + 1) * 128],
                              Cb[:, kb * THT + ro:kb * THT + ro + n],
                              start=st, stop=sp_),
                          reads=wik + Ctx.rkeys("C", kb * THT + ro, n, 2),
                          writes=(psk,))
                d0 = XC0 + b * LPC + o
                p.add("act", lambda pr, d0=d0, n=n, ps=ps: pr.activation(
                    out=Bf[:, d0:d0 + n], in_=ps[:, 0:n], func=AF.Silu),
                    reads=(psk,), writes=xck)
        p.dma("act", szd[:, gz, :], Bf[:, XC0:XC0 + T], reads=xck,
              writes=(f"szd{gz}",), mark=True)

    DT0 = DB * THT + T
    dtk = Ctx.rkeys("C", DT0, T, 2)
    BC0 = XC0 + T
    bck = Ctx.rkeys("B", BC0, T, 4)
    for ci, (o, n) in enumerate(chunks(T, 512)):
        p.add("act", lambda pr, o=o, n=n, ci=ci: pr.copy(
            out=Cb[0:64, DT0 + o:DT0 + o + n], in_=ctx.PS[ci][0:64, 0:n]),
            reads=(f"ps{ci}",), writes=dtk)
        p.add("act", lambda pr, o=o, n=n, ci=ci: pr.copy(
            out=Bf[64:96, BC0 + o:BC0 + o + n], in_=ctx.PS[ci][64:96, 0:n]),
            reads=(f"ps{ci}",), writes=bck)
        p.dma("act", bcd[:, o:o + n], Bf[64:96, BC0 + o:BC0 + o + n],
              reads=bck, writes=("bcd",), mark=True)

    R0 = BC0 + T
    A0 = R0 + 1024
    A1 = A0 + 1024
    D0 = A1 + 1024
    rrk = Ctx.rkeys("B", R0, 1024, 4)
    aak = Ctx.rkeys("B", A0, 1024, 4)
    a1k = Ctx.rkeys("B", A1, 1024, 4)
    dlk = Ctx.rkeys("B", D0, T, 4)
    p.add("dve", lambda pr: pr.memset(ctx.cst[:, 2:3], -1.0), writes=("cstm1",))
    for eb in range(EG):
        for b in range(B):
            for ci, (o, n) in enumerate(chunks(LPC, 512)):
                pi = 4 + (ngrp % 2)
                ngrp += 1
                ps, psk = ctx.PS[pi], f"ps{pi}"
                p.add("pe", lambda pr, eb=eb, b=b, o=o, n=n, ps=ps:
                      pr.matmul(ps[:, 0:n],
                                wdt[:, eb * 128:(eb + 1) * 128],
                                Cb[0:64, DT0 + b * LPC + o:DT0 + b * LPC + o + n],
                                start=True, stop=True),
                      reads=dtk + ("wdt",), writes=(psk,))
                p.add("act", lambda pr, eb=eb, o=o, n=n, ps=ps: pr.activation(
                    out=Bf[:, R0 + o:R0 + o + n], in_=ps[:, 0:n], func=AF.Relu,
                    bias=bdt[:, eb:eb + 1]),
                    reads=(psk, "b_dt"), writes=rrk)
                p.add("act", lambda pr, eb=eb, o=o, n=n, ps=ps: pr.activation(
                    out=Bf[:, A0 + o:A0 + o + n], in_=ps[:, 0:n], func=AF.Abs,
                    bias=bdt[:, eb:eb + 1]),
                    reads=(psk, "b_dt"), writes=aak)
            p.add("act", lambda pr: pr.activation(
                out=Bf[:, A1:A1 + 1024], in_=Bf[:, A0:A0 + 1024], func=AF.Exp,
                scale=ctx.cst[:, 2:3]),
                reads=aak + ("cstm1",), writes=a1k)
            p.add("act", lambda pr: pr.activation(
                out=Bf[:, A0:A0 + 1024], in_=Bf[:, A1:A1 + 1024], func=AF.Ln,
                bias=ctx.cst[:, 0:1]),
                reads=a1k + ("cst",), writes=aak)
            sc = 256 + eb
            p.add("dve", lambda pr, b=b, sc=sc: pr.scalar_tensor_tensor(
                out=Bf[:, D0 + b * LPC:D0 + b * LPC + LPC],
                in0=Bf[:, R0:R0 + 1024], scalar=1.0,
                in1=Bf[:, A0:A0 + 1024], op0=ALU.mult, op1=ALU.add,
                accum_out=stt[:, b * 272 + sc:b * 272 + sc + 1]),
                reads=rrk + aak, writes=dlk + ("stt",))
        p.dma("act", dtd[:, eb, :], Bf[:, D0:D0 + T], reads=dlk,
              writes=(f"dtd{eb}",), mark=True)

    SD = 0
    sdel = (SD, SD + 1024, SD + 2048, SD + 3072, SD + 4096, SD + 5120)
    dK = [Ctx.rkeys("B", c, 1024, 4) for c in sdel]
    for b in range(B):
        for gb in range(EG):
            p.dma("sp", Bf[:, sdel[0]:sdel[0] + 1024],
                  dtd[:, gb, b * LPC:(b + 1) * LPC], reads=(f"dtd{gb}",),
                  writes=dK[0], tag="dl_ld")
            p.dma("sp", Bf[:, sdel[1]:sdel[1] + 1024],
                  xcd[:, gb, b * LPC:(b + 1) * LPC], reads=(f"xcd{gb}",),
                  writes=dK[1], tag="xc_ld")
            p.add("dve", lambda pr: pr.tensor_mul(
                Bf[:, sdel[2]:sdel[2] + 1024], Bf[:, sdel[0]:sdel[0] + 1024],
                Bf[:, sdel[1]:sdel[1] + 1024]),
                reads=dK[0] + dK[1], writes=dK[2])
            for n_ in range(NST):
                p.add("act", lambda pr, gb=gb, n_=n_: pr.activation(
                    out=Bf[:, sdel[3]:sdel[3] + 1024],
                    in_=Bf[:, sdel[0]:sdel[0] + 1024], func=AF.Exp,
                    scale=att[:, gb * NST + n_:gb * NST + n_ + 1]),
                    reads=dK[0] + ("At",), writes=dK[3])
                p.dma("sp", rs_bc[:, 0:512],
                      bcd[n_:n_ + 1, b * LPC:b * LPC + 512]
                      .to_broadcast([128, 512]), reads=("bcd",),
                      writes=("l0rs",), tag="bbc")
                p.dma("sp", mr_bc[:, 0:512],
                      bcd[n_:n_ + 1, b * LPC + 512:(b + 1) * LPC]
                      .to_broadcast([128, 512]), reads=("bcd",),
                      writes=("l0mr",), tag="bbc")
                p.add("dve", lambda pr: pr.tensor_mul(
                    Bf[:, sdel[4]:sdel[4] + 512], Bf[:, sdel[2]:sdel[2] + 512],
                    rs_bc[:, 0:512]), reads=dK[2] + ("l0rs",), writes=dK[4])
                p.add("dve", lambda pr: pr.tensor_mul(
                    Bf[:, sdel[4] + 512:sdel[4] + 1024],
                    Bf[:, sdel[2] + 512:sdel[2] + 1024], mr_bc[:, 0:512]),
                    reads=dK[2] + ("l0mr",), writes=dK[4])
                p.add("dve", lambda pr: pr.tensor_tensor_scan(
                    out=Bf[:, sdel[5]:sdel[5] + 1024],
                    data0=Bf[:, sdel[3]:sdel[3] + 1024],
                    data1=Bf[:, sdel[4]:sdel[4] + 1024], initial=0.0,
                    op0=ALU.mult, op1=ALU.add),
                    reads=dK[3] + dK[4], writes=dK[5])
                sc = b * 272 + gb * NST + n_
                p.add("dve", lambda pr, sc=sc: pr.tensor_copy(
                    out=stt[:, sc:sc + 1],
                    in_=Bf[:, sdel[5] + 1023:sdel[5] + 1024]),
                    sync_reads=dK[5], writes=("stt",))
    p.dma("sp", st1[:, :], stt[:, :], reads=("stt",), writes=("st1",),
          mark=True)
    return g


def build_l0a2(nc, ctx):
    p = ctx.prog
    g = {}
    g["xsin"] = nc.dram_tensor("xsin", [128, DB, THT], F32,
                               kind="ExternalInput").ap()
    for nm, shp in (("xcd", [128, EG, T]), ("szd", [128, EG, T]),
                    ("dtd", [128, EG, T]), ("bcd", [32, T]),
                    ("rowsa", [2, THT]), ("states", [NCORES, 128, 544]),
                    ("cmask", [128, 8]), ("A_log", [128, EG * NST]),
                    ("D_skip", [128, EG]), ("W_outT", [ED, D]),
                    ("rms2_w", [128, DB]), ("W_routerT", [D, 8]),
                    ("gate_wT", [NEXP, D, FF]), ("up_wT", [NEXP, D, FF]),
                    ("down_wT", [NEXP, FF, D])):
        g[nm] = nc.dram_tensor(nm, shp, F32, kind="ExternalInput").ap()
    xso = nc.dram_tensor("xso", [128, DB, T], F32, kind="ExternalOutput").ap()

    A, Ab, Bf, Bb, Cb = ctx.A, ctx.Ab, ctx.Bp, ctx.Bb, ctx.Cb
    att = nc.alloc_sbuf_tensor("l2_at", [128, EG * NST], F32)
    dsk = nc.alloc_sbuf_tensor("l2_dsk", [128, EG], F32)
    rms2 = nc.alloc_sbuf_tensor("l2_rms2", [128, DB], F32)
    cmk = nc.alloc_sbuf_tensor("l2_cmk", [128, 8], F32)
    hin = nc.alloc_sbuf_tensor("l2_hin", [128, 512], F32)
    stj = nc.alloc_sbuf_tensor("l2_stj", [128, 544], F32)
    sef = nc.alloc_sbuf_tensor("l2_sef", [128, 16], F32)
    ptl = nc.alloc_sbuf_tensor("l2_ptl", [128, 16], F32)
    rwt = nc.alloc_sbuf_tensor("l2_rwt", [128, 64], BF16)
    tmp = nc.alloc_sbuf_tensor("l2_tmp", [128, 512], F32)
    gsc = nc.alloc_sbuf_tensor("l2_gsc", [128, 512], F32)
    wm_bc = nc.alloc_sbuf_tensor("l2_wmbc", [128, 512], F32)
    rs_bc = nc.alloc_sbuf_tensor("l2_rs", [128, 512], F32)
    mr_bc = nc.alloc_sbuf_tensor("l2_mr", [128, 512], F32)
    xrl = nc.alloc_sbuf_tensor("l2_xrl", [128, 512], F32)
    idf8 = nc.alloc_sbuf_tensor("l2_idf8", [8, 8], F32)
    idf128 = nc.alloc_sbuf_tensor("l2_idf", [128, 128], F32)
    wmd = nc.dram_tensor("l2_wmd", [8, T], F32, kind="Internal")
    g_id8 = nc.inline_tensor(np.eye(8, dtype=np.float32), name="id8c2").ap()
    g_idf = nc.inline_tensor(np.eye(128, dtype=np.float32), name="idfc2").ap()
    p.dma("sp", idf8[:, :], g_id8, writes=("id8",))
    p.dma("sp", idf128[:, :], g_idf, writes=("idf",))
    p.dma("sp", dsk[:, :], g["D_skip"], writes=("dsk",))
    p.dma("sp", rms2[:, :], g["rms2_w"], writes=("rms2",))
    p.dma("sp", cmk[:, :], g["cmask"], writes=("cmk",))
    p.dma("sp", ctx.rows_dram.ap()[0:2, 0:THT], g["rowsa"], writes=("rows:0d",))
    p.dma("sp", att[:, :], g["A_log"], writes=("Atl",))
    att2 = nc.alloc_sbuf_tensor("l2_at2", [128, EG * NST], F32)
    p.add("act", lambda pr: pr.activation(out=att2[:, :], in_=att[:, :],
                                          func=AF.Exp),
          reads=("Atl",), writes=("At2",))
    p.add("dve", lambda pr: pr.tensor_scalar_mul(att[:, :], att2[:, :], -1.0),
          reads=("At2",), writes=("At",))

    p.add("dve", lambda pr: pr.memset(hin[:, :], 0.0), writes=("hin",))
    for j in range(NCORES - 1):
        p.dma("sp", stj[:, :], g["states"][j, :, :], writes=("stj",),
              tag="stj")
        for b in range(B):
            p.add("dve", lambda pr, b=b, j=j: pr.tensor_scalar_mul(
                sef[:, :], stj[:, b * 272 + 256:b * 272 + 272],
                cmk[:, j:j + 1]),
                reads=("stj", "cmk"), writes=("sef",))
            for gb in range(EG):
                p.add("act", lambda pr, gb=gb: pr.activation(
                    out=ptl[:, :], in_=att[:, gb * NST:(gb + 1) * NST],
                    func=AF.Exp, scale=sef[:, gb:gb + 1]),
                    reads=("At",), sync_reads=("sef",), writes=("ptl",))
                h0 = b * 256 + gb * NST
                p.add("dve", lambda pr, h0=h0: pr.tensor_mul(
                    hin[:, h0:h0 + NST], hin[:, h0:h0 + NST], ptl[:, :]),
                    sync_reads=("hin", "ptl"), writes=("hin",))
                p.add("dve", lambda pr, h0=h0, b=b, gb=gb, j=j:
                      pr.scalar_tensor_tensor(
                          out=hin[:, h0:h0 + NST],
                          in0=stj[:, b * 272 + gb * NST:b * 272 + (gb + 1) * NST],
                          scalar=cmk[:, j:j + 1], in1=hin[:, h0:h0 + NST],
                          op0=ALU.mult, op1=ALU.add),
                      reads=("stj", "cmk"), sync_reads=("hin",),
                      writes=("hin",))

    sdel = (0, 1024, 2048, 3072, 4096, 5120, 6144, 7168)
    dK = [Ctx.rkeys("B", c, 1024, 4) for c in sdel]
    for b in range(B):
        for gb in range(EG):
            p.dma("sp", Bf[:, sdel[0]:sdel[0] + 1024],
                  g["dtd"][:, gb, b * LPC:(b + 1) * LPC], writes=dK[0],
                  tag="dl_ld")
            p.dma("sp", Bf[:, sdel[1]:sdel[1] + 1024],
                  g["xcd"][:, gb, b * LPC:(b + 1) * LPC], writes=dK[1],
                  tag="xc_ld")
            p.add("dve", lambda pr: pr.tensor_mul(
                Bf[:, sdel[2]:sdel[2] + 1024], Bf[:, sdel[0]:sdel[0] + 1024],
                Bf[:, sdel[1]:sdel[1] + 1024]),
                reads=dK[0] + dK[1], writes=dK[2])
            p.add("dve", lambda pr, gb=gb: pr.tensor_scalar_mul(
                Bf[:, sdel[6]:sdel[6] + 1024], Bf[:, sdel[1]:sdel[1] + 1024],
                dsk[:, gb:gb + 1]),
                reads=dK[1] + ("dsk",), writes=dK[6])
            for n_ in range(NST):
                p.add("act", lambda pr, gb=gb, n_=n_: pr.activation(
                    out=Bf[:, sdel[3]:sdel[3] + 1024],
                    in_=Bf[:, sdel[0]:sdel[0] + 1024], func=AF.Exp,
                    scale=att[:, gb * NST + n_:gb * NST + n_ + 1]),
                    reads=dK[0] + ("At",), writes=dK[3])
                p.dma("sp", rs_bc[:, 0:512],
                      g["bcd"][n_:n_ + 1, b * LPC:b * LPC + 512]
                      .to_broadcast([128, 512]), writes=("l2br",), tag="bbc")
                p.dma("sp", mr_bc[:, 0:512],
                      g["bcd"][n_:n_ + 1, b * LPC + 512:(b + 1) * LPC]
                      .to_broadcast([128, 512]), writes=("l2bm",), tag="bbc")
                p.add("dve", lambda pr: pr.tensor_mul(
                    Bf[:, sdel[4]:sdel[4] + 512], Bf[:, sdel[2]:sdel[2] + 512],
                    rs_bc[:, 0:512]), reads=dK[2] + ("l2br",), writes=dK[4])
                p.add("dve", lambda pr: pr.tensor_mul(
                    Bf[:, sdel[4] + 512:sdel[4] + 1024],
                    Bf[:, sdel[2] + 512:sdel[2] + 1024], mr_bc[:, 0:512]),
                    reads=dK[2] + ("l2bm",), writes=dK[4])
                h0 = b * 256 + gb * NST + n_
                p.add("dve", lambda pr, h0=h0: pr.tensor_tensor_scan(
                    out=Bf[:, sdel[5]:sdel[5] + 1024],
                    data0=Bf[:, sdel[3]:sdel[3] + 1024],
                    data1=Bf[:, sdel[4]:sdel[4] + 1024],
                    initial=hin[:, h0:h0 + 1],
                    op0=ALU.mult, op1=ALU.add),
                    reads=dK[3] + dK[4] + ("hin",), writes=dK[5])
                p.dma("sp", rs_bc[:, 0:512],
                      g["bcd"][16 + n_:17 + n_, b * LPC:b * LPC + 512]
                      .to_broadcast([128, 512]), writes=("l2br",), tag="cbc")
                p.dma("sp", mr_bc[:, 0:512],
                      g["bcd"][16 + n_:17 + n_, b * LPC + 512:(b + 1) * LPC]
                      .to_broadcast([128, 512]), writes=("l2bm",), tag="cbc")
                p.add("dve", lambda pr: pr.tensor_mul(
                    Bf[:, sdel[7]:sdel[7] + 512], Bf[:, sdel[5]:sdel[5] + 512],
                    rs_bc[:, 0:512]), reads=dK[5] + ("l2br",), writes=dK[7])
                p.add("dve", lambda pr: pr.tensor_mul(
                    Bf[:, sdel[7] + 512:sdel[7] + 1024],
                    Bf[:, sdel[5] + 512:sdel[5] + 1024], mr_bc[:, 0:512]),
                    reads=dK[5] + ("l2bm",), writes=dK[7])
                p.add("dve", lambda pr: pr.tensor_add(
                    Bf[:, sdel[6]:sdel[6] + 1024], Bf[:, sdel[6]:sdel[6] + 1024],
                    Bf[:, sdel[7]:sdel[7] + 1024]),
                    reads=dK[7] + dK[6], writes=dK[6])
            p.dma("sp", Bf[:, sdel[1]:sdel[1] + 1024],
                  g["szd"][:, gb, b * LPC:(b + 1) * LPC], writes=dK[1],
                  tag="sz_ld")
            yc = b * 16384 + gb * LPC
            p.add("dve", lambda pr, yc=yc: pr.tensor_mul(
                Cb[:, yc:yc + LPC], Bf[:, sdel[6]:sdel[6] + 1024],
                Bf[:, sdel[1]:sdel[1] + 1024]),
                reads=dK[6] + dK[1], writes=Ctx.rkeys("C", yc, LPC, 2))

    wok = Ctx.rkeys("B", 8192, 16384, 2)
    for gb in range(EG):
        load_w_bf(ctx, g["W_outT"][gb * 128:(gb + 1) * 128, :], 128, D,
                  Bb, 8192 + gb * D, wok, q="act", stg_half=gb % 2)
    for b in range(B):
        for ci, (o, n) in enumerate(chunks(LPC, 512)):
            for db in range(DB):
                pi = 4 + ((ci * DB + db) % 2)
                ps, psk = ctx.PS[pi], f"ps{pi}"
                for gb in range(EG):
                    st, sp_ = (gb == 0), (gb == EG - 1)
                    yc = b * 16384 + gb * LPC + o
                    p.add("pe", lambda pr, gb=gb, db=db, yc=yc, n=n, st=st,
                          sp_=sp_, ps=ps: pr.matmul(
                              ps[:, 0:n],
                              Bb[:, 8192 + gb * D + db * 128:
                                 8192 + gb * D + (db + 1) * 128],
                              Cb[:, yc:yc + n], start=st, stop=sp_),
                          reads=wok + Ctx.rkeys("C", yc, n, 2), writes=(psk,))
                ro = b * TH + HALO + o
                p.dma("sp", xrl[:, 0:n], g["xsin"][:, db, ro:ro + n],
                      writes=("l2xrl",), tag="x0ld")
                bcast_rows(ctx, 0, 1, ro, n, rs_bc, ("l2br",))
                bcast_rows(ctx, 0, 0, ro, n, mr_bc, ("l2bm",))
                uc = db * T + b * LPC + o
                uk = Ctx.rkeys("A", uc, n, 4)
                p.add("dve", lambda pr, n=n: pr.tensor_mul(
                    tmp[:, 0:n], xrl[:, 0:n], rs_bc[:, 0:n]),
                    reads=("l2xrl", "l2br"), writes=("l2_tmp",))
                p.add("dve", lambda pr, n=n: pr.tensor_sub(
                    tmp[:, 0:n], tmp[:, 0:n], mr_bc[:, 0:n]),
                    reads=("l2_tmp", "l2bm"), writes=("l2_tmp",))
                p.add("dve", lambda pr, uc=uc, n=n, ps=ps: pr.tensor_add(
                    A[:, uc:uc + n], tmp[:, 0:n], ps[:, 0:n]),
                    reads=("l2_tmp", psk), writes=uk)

    def u_bf(i):
        return Cb[:, i * T:(i + 1) * T], Ctx.rkeys("C", i * T, T, 2)

    for i in range(DB):
        p.add("dve", lambda pr, i=i: pr.tensor_copy(
            out=Cb[:, i * T:(i + 1) * T], in_=A[:, i * T:(i + 1) * T]),
            reads=Ctx.rkeys("A", i * T, T, 4),
            writes=Ctx.rkeys("C", i * T, T, 2))
    sqt = Cb[:, 16384:16384 + T]
    sqk = Ctx.rkeys("C", 16384, T, 2)
    ln_stats(ctx, "rms2", u_bf, DB, T, 2, RMS_EPS, False, 0, 1, sqt, sqk)
    for i in range(DB):
        for (o, n) in chunks(T, 512):
            bcast_rows(ctx, 2, 1, o, n, rs_bc, ("l2br",))
            p.add("dve", lambda pr, i=i, o=o, n=n: pr.tensor_mul(
                tmp[:, 0:n], A[:, i * T + o:i * T + o + n], rs_bc[:, 0:n]),
                reads=Ctx.rkeys("A", i * T + o, n, 4) + ("l2br",),
                writes=("l2_tmp",))
            p.add("dve", lambda pr, i=i, o=o, n=n: pr.tensor_scalar_mul(
                Cb[:, i * T + o:i * T + o + n], tmp[:, 0:n],
                rms2[:, i:i + 1]),
                reads=("l2_tmp", "rms2"),
                writes=Ctx.rkeys("C", i * T + o, n, 2))

    for kb in range(DB):
        load_w_bf(ctx, g["W_routerT"][kb * 128:(kb + 1) * 128, :], 128, 8,
                  rwt, kb * 8, ("rwt",), q="act", stg_half=kb % 2)
    LG = 0
    lgk = Ctx.rkeys("B", LG, 2048, 4)
    for ci, (o, n) in enumerate(chunks(T, 512)):
        ps, psk = ctx.PS[4 + ci % 2], f"ps{4 + ci % 2}"
        for kb in range(DB):
            st, sp_ = (kb == 0), (kb == DB - 1)
            p.add("pe", lambda pr, kb=kb, o=o, n=n, st=st, sp_=sp_, ps=ps:
                  pr.matmul(ps[0:8, 0:n], rwt[:, kb * 8:(kb + 1) * 8],
                            Cb[:, kb * T + o:kb * T + o + n],
                            start=st, stop=sp_),
                  reads=("rwt",) + Ctx.rkeys("C", kb * T + o, n, 2),
                  writes=(psk,))
        p.add("dve", lambda pr, o=o, n=n, ps=ps: pr.tensor_copy(
            out=Bf[0:8, LG + o:LG + o + n], in_=ps[0:8, 0:n]),
            reads=(psk,), writes=lgk)
    WMT = 2048
    wmk = Ctx.rkeys("B", WMT, 2048, 4)
    lgt = Bf[0:128, 4096:4096 + 8]
    ltk = Ctx.rkeys("B", 4096, 40, 4)
    top8 = Bf[0:128, 4104:4104 + 8]
    sel8 = Bf[0:128, 4112:4112 + 8]
    for t16 in range(16):
        c0 = t16 * 128
        p.add("pe", lambda pr, c0=c0: pr.transpose(
            ctx.PS[0][:, 0:8], Bf[0:8, LG + c0:LG + c0 + 128], idf8[:, :]),
            reads=lgk + ("id8",), writes=("ps0",))
        p.add("dve", lambda pr: pr.tensor_copy(out=lgt[:, :],
                                               in_=ctx.PS[0][:, 0:8]),
              reads=("ps0",), writes=ltk)
        p.add("dve", lambda pr: pr.reduce_max(
            out=Bf[0:128, 4120:4121], in_=lgt[:, :], axis=AX),
            sync_reads=ltk, writes=ltk)
        p.add("dve", lambda pr: pr.tensor_scalar_mul(
            Bf[0:128, 4121:4122], Bf[0:128, 4120:4121], -1.0),
            sync_reads=ltk, writes=ltk)
        p.add("act", lambda pr: pr.activation(
            out=lgt[:, :], in_=lgt[:, :], func=AF.Exp,
            bias=Bf[0:128, 4121:4122]),
            sync_reads=ltk, writes=ltk)
        p.add("dve", lambda pr: pr.reduce_sum(
            out=Bf[0:128, 4120:4121], in_=lgt[:, :], axis=AX),
            sync_reads=ltk, writes=ltk)
        p.add("dve", lambda pr: pr.reciprocal(
            Bf[0:128, 4121:4122], Bf[0:128, 4120:4121]),
            sync_reads=ltk, writes=ltk)
        p.add("dve", lambda pr: pr.tensor_scalar_mul(
            lgt[:, :], lgt[:, :], Bf[0:128, 4121:4122]),
            sync_reads=ltk, writes=ltk)
        p.add("dve", lambda pr: pr.max(out=top8[:, :], in_=lgt[:, :]),
              sync_reads=ltk, writes=ltk)
        p.add("dve", lambda pr: pr.tensor_scalar(
            out=sel8[:, :], in0=lgt[:, :], scalar1=top8[:, 1:2], scalar2=None,
            op0=ALU.is_ge), sync_reads=ltk, writes=ltk)
        p.add("dve", lambda pr: pr.tensor_mul(lgt[:, :], lgt[:, :],
                                              sel8[:, :]),
              sync_reads=ltk, writes=ltk)
        p.add("pe", lambda pr: pr.transpose(
            ctx.PS[1][0:8, 0:128], lgt[:, :], idf128[:, :]),
            sync_reads=ltk, reads=("idf",), writes=("ps1",))
        p.add("dve", lambda pr, c0=c0: pr.tensor_copy(
            out=Bf[0:8, WMT + c0:WMT + c0 + 128], in_=ctx.PS[1][0:8, 0:128]),
            reads=("ps1",), writes=wmk)
    p.dma("sp", wmd.ap()[:, :], Bf[0:8, WMT:WMT + T], reads=wmk,
          writes=("wmd",))

    for e in range(NEXP):
        for half in range(2):
            gk = Ctx.rkeys("B", 0, 8192, 2)
            uk_ = Ctx.rkeys("B", 8192, 8192, 2)
            dk_ = Ctx.rkeys("B", 16384, 8192, 2)
            for kb in range(DB):
                load_w_bf(ctx, g["gate_wT"][e, kb * 128:(kb + 1) * 128,
                                            half * 1024:(half + 1) * 1024],
                          128, 1024, Bb, kb * 1024, gk, q="act",
                          stg_half=kb % 2)
                load_w_bf(ctx, g["up_wT"][e, kb * 128:(kb + 1) * 128,
                                          half * 1024:(half + 1) * 1024],
                          128, 1024, Bb, 8192 + kb * 1024, uk_, q="act",
                          stg_half=kb % 2)
            for fb in range(DB):
                load_w_bf(ctx, g["down_wT"][e, half * 1024 + fb * 128:
                                            half * 1024 + (fb + 1) * 128, :],
                          128, D, Bb, 16384 + fb * D, dk_, q="act",
                          stg_half=fb % 2)
            for (o, n) in chunks(T, 512):
                p.dma("sp", wm_bc[:, 0:n],
                      wmd.ap()[e:e + 1, o:o + n].to_broadcast([128, n]),
                      reads=("wmd",), writes=("wmbc",))
                h2wk = Ctx.rkeys("C", 16384, 4096, 2)
                for kb in range(DB):
                    p.add("dve", lambda pr, kb=kb, o=o, n=n: pr.tensor_mul(
                        Cb[:, 16384 + kb * 512:16384 + kb * 512 + n],
                        Cb[:, kb * T + o:kb * T + o + n], wm_bc[:, 0:n]),
                        reads=Ctx.rkeys("C", kb * T + o, n, 2) + ("wmbc",),
                        writes=h2wk)
                hk = Ctx.rkeys("C", 20480, 4096, 2)
                for fb in range(DB):
                    psg, psgk = ctx.PS[4], "ps4"
                    psu, psuk = ctx.PS[5], "ps5"
                    for kb in range(DB):
                        st, sp_ = (kb == 0), (kb == DB - 1)
                        p.add("pe", lambda pr, kb=kb, fb=fb, o=o, n=n, st=st,
                              sp_=sp_: pr.matmul(
                                  ctx.PS[4][:, 0:n],
                                  Bb[:, kb * 1024 + fb * 128:
                                     kb * 1024 + (fb + 1) * 128],
                                  Cb[:, kb * T + o:kb * T + o + n],
                                  start=st, stop=sp_),
                              reads=gk + Ctx.rkeys("C", kb * T + o, n, 2),
                              writes=(psgk,))
                    for kb in range(DB):
                        st, sp_ = (kb == 0), (kb == DB - 1)
                        p.add("pe", lambda pr, kb=kb, fb=fb, o=o, n=n, st=st,
                              sp_=sp_: pr.matmul(
                                  ctx.PS[5][:, 0:n],
                                  Bb[:, 8192 + kb * 1024 + fb * 128:
                                     8192 + kb * 1024 + (fb + 1) * 128],
                                  Cb[:, 16384 + kb * 512:16384 + kb * 512 + n],
                                  start=st, stop=sp_),
                              reads=uk_ + h2wk, writes=(psuk,))
                    p.add("act", lambda pr, n=n: pr.activation(
                        out=gsc[:, 0:n], in_=ctx.PS[4][:, 0:n], func=AF.Silu),
                        reads=(psgk,), writes=("l2_gsc",))
                    p.add("dve", lambda pr, fb=fb, n=n: pr.tensor_mul(
                        Cb[:, 20480 + fb * 512:20480 + fb * 512 + n],
                        gsc[:, 0:n], ctx.PS[5][:, 0:n]),
                        reads=("l2_gsc", psuk), writes=hk)
                for db in range(DB):
                    ps, psk = ctx.PS[db % 2], f"ps{db % 2}"
                    for fb in range(DB):
                        st, sp_ = (fb == 0), (fb == DB - 1)
                        p.add("pe", lambda pr, db=db, fb=fb, n=n, st=st,
                              sp_=sp_, ps=ps: pr.matmul(
                                  ps[:, 0:n],
                                  Bb[:, 16384 + fb * D + db * 128:
                                     16384 + fb * D + (db + 1) * 128],
                                  Cb[:, 20480 + fb * 512:20480 + fb * 512 + n],
                                  start=st, stop=sp_),
                              reads=dk_ + hk, writes=(psk,))
                    ak = Ctx.rkeys("A", db * T + o, n, 4)
                    p.add("dve", lambda pr, db=db, o=o, n=n, ps=ps:
                          pr.tensor_add(A[:, db * T + o:db * T + o + n],
                                        A[:, db * T + o:db * T + o + n],
                                        ps[:, 0:n]),
                          reads=(psk,) + ak, writes=ak)

    for i in range(DB):
        p.add("dve", lambda pr, i=i: pr.tensor_copy(
            out=Cb[:, i * T:(i + 1) * T], in_=A[:, i * T:(i + 1) * T]),
            reads=Ctx.rkeys("A", i * T, T, 4),
            writes=Ctx.rkeys("C", i * T, T, 2))
    ln_stats(ctx, "lnb", u_bf, DB, T, 4, LN_EPS, True, 0, 1, sqt, sqk)
    for i in range(DB):
        for b in range(B):
            for ci, (o, n) in enumerate(chunks(LPC, 512)):
                uc = i * T + b * LPC + o
                ak = Ctx.rkeys("A", uc, n, 4)
                ro = b * TH + HALO + o
                p.dma("sp", xrl[:, 0:n], g["xsin"][:, i, ro:ro + n],
                      writes=("l2xrl",), tag="xreload")
                bcast_rows(ctx, 4, 1, b * LPC + o, n, rs_bc, ("l2br",))
                bcast_rows(ctx, 4, 0, b * LPC + o, n, mr_bc, ("l2bm",))
                p.add("dve", lambda pr, uc=uc, n=n: pr.tensor_mul(
                    A[:, uc:uc + n], A[:, uc:uc + n], rs_bc[:, 0:n]),
                    reads=ak + ("l2br",), writes=ak)
                p.add("dve", lambda pr, uc=uc, n=n: pr.tensor_sub(
                    A[:, uc:uc + n], A[:, uc:uc + n], mr_bc[:, 0:n]),
                    reads=ak + ("l2bm",), writes=ak)
                p.add("dve", lambda pr, uc=uc, n=n: pr.tensor_add(
                    A[:, uc:uc + n], A[:, uc:uc + n], xrl[:, 0:n]),
                    reads=ak + ("l2xrl",), writes=ak)
                p.dma("act", xso[:, i, b * LPC + o:b * LPC + o + n],
                      A[:, uc:uc + n], reads=ak, writes=(f"xso{i}_{uc}",),
                      tag="xso", mark=True)
    return g


def run_l0a(percore, W):
    nc = new_launch()
    ctx = Ctx(nc)
    build_l0a1(nc, ctx)
    in_maps = []
    for c in range(NCORES):
        m = {"xsin": percore[c]["xs_in"].reshape(128, DB, THT),
             "rms1_w": W["rms1_w"], "W_inT": W["W_inT"],
             "conv_w": W["conv_w"], "conv_b": W["conv_b"],
             "W_xprojT": W["W_xprojT"], "W_dtT": W["W_dtT"],
             "b_dt": W["b_dt"], "A_log": W["A_log"]}
        in_maps.append(m)
    res = run_launch(nc, ctx, in_maps)
    states = np.ascontiguousarray(
        np.stack([res.results[c]["st1"] for c in range(NCORES)]))

    nc2 = new_launch()
    ctx2 = Ctx(nc2)
    build_l0a2(nc2, ctx2)
    in2 = []
    for c in range(NCORES):
        r = res.results[c]
        m = {"xsin": percore[c]["xs_in"].reshape(128, DB, THT),
             "xcd": r["xcd"], "szd": r["szd"], "dtd": r["dtd"],
             "bcd": r["bcd"], "rowsa": r["rowsa"], "states": states,
             "cmask": percore[c]["cmask"], "A_log": W["A_log"],
             "D_skip": W["D_skip"], "W_outT": W["W_outT"],
             "rms2_w": W["rms2_w"], "W_routerT": W["W_routerT"],
             "gate_wT": W["gate_wT"], "up_wT": W["up_wT"],
             "down_wT": W["down_wT"]}
        in2.append(m)
    res2 = run_launch(nc2, ctx2, in2)
    return [r["xso"] for r in res2.results]


KERNEL_STAGES = ["l0a1", "l0a2", "l0f", "l1a1", "l1a2", "l1f"]


def kernel(**inputs):
    W, percore = prep_host(inputs)
    xs = run_l0a(percore, W)
    xs = run_l0f(xs, W)
    xs = run_l1a(xs, W)
    xs = run_l1f(xs, W)
    return finish_output(xs)
